# version 03 (final)
# baseline (speedup 1.0000x reference)
import contextlib
import numpy as np
import concourse.bass as bass
import concourse.mybir as mybir
from concourse.bass_utils import run_bass_kernel_spmd

F32 = mybir.dt.float32
I32 = mybir.dt.int32
BF16 = mybir.dt.bfloat16
AF = mybir.ActivationFunctionType
OP = mybir.AluOpType
AX = mybir.AxisListType

NCORES = 8
B = 4
S = 4096
D = 1024
PPC = S // NCORES
TPC = B * PPC
NT = TPC // 128
TPB = NT // B
NH = 16
HD = 64
NCH = D // 128

ENGS = ("tensor", "vector", "scalar", "gpsimd", "sync")
ENG_TAG = {"tensor": "t", "vector": "v", "scalar": "s", "gpsimd": "g", "sync": None}
DMA_TAG = {"sync": "dq", "gpsimd": "dg", "scalar": "ds", "vector": "dv", "tensor": None}
ALL_TAGS = ("t", "v", "s", "g", "dq", "dg", "ds", "dv", "cc")
RG = [list(range(NCORES))]


class Prog:
    def __init__(self, nc):
        self.nc = nc
        self.stack = contextlib.ExitStack()
        self.sem = {tag: self.stack.enter_context(nc.semaphore("sem_" + tag)) for tag in ALL_TAGS}
        self.tot = {tag: 0 for tag in ALL_TAGS}
        self.barv = {tag: 0 for tag in ALL_TAGS}
        self.last_wait = {}
        self.ops = {e: [] for e in ENGS}

    def sbuf(self, name, shape, dtype=F32):
        return self.stack.enter_context(self.nc.sbuf_tensor(name, shape, dtype))

    def psum(self, name, shape, dtype=F32):
        return self.stack.enter_context(self.nc.psum_tensor(name, shape, dtype))

    def scope(self):
        return _Scope(self)

    def bar(self):
        self.barv = dict(self.tot)

    def _rec(self, eng_name, fn, inc_tag, step):
        waits = []
        for tag, need in self.barv.items():
            if need > self.last_wait.get((eng_name, tag), 0):
                waits.append((tag, need))
                self.last_wait[(eng_name, tag)] = need
        self.ops[eng_name].append((fn, waits, inc_tag, step))
        if inc_tag is not None:
            self.tot[inc_tag] += step

    def em(self, eng_name, fname, *args, inc=True, **kw):
        tag = ENG_TAG[eng_name] if inc else None
        self._rec(eng_name, lambda eng: getattr(eng, fname)(*args, **kw), tag, 1)

    def dma(self, eng_name, out, in_, **kw):
        tag = DMA_TAG[eng_name]
        self._rec(eng_name, lambda eng: eng.dma_start(out=out, in_=in_, **kw), tag, 16)

    def cc(self, kind, op, ins, outs, **kw):
        prev = self.tot["cc"]
        if prev > self.last_wait.get(("gpsimd", "cc"), 0):
            self.ops["gpsimd"].append(
                ((lambda v: (lambda eng: eng.wait_ge(self.sem["cc"], v)))(prev), [], None, 0))
            self.last_wait[("gpsimd", "cc")] = prev
        self._rec("gpsimd",
                  lambda eng: eng.collective_compute(kind, op, replica_groups=RG,
                                                     ins=ins, outs=outs, **kw),
                  "cc", 1)

    def t(self, fname, *a, **kw):
        self.em("tensor", fname, *a, **kw)

    def v(self, fname, *a, **kw):
        self.em("vector", fname, *a, **kw)

    def s(self, fname, *a, **kw):
        self.em("scalar", fname, *a, **kw)

    def g(self, fname, *a, **kw):
        self.em("gpsimd", fname, *a, **kw)

    def act(self, out, in_, func, **kw):
        self.em("scalar", "activation", out, in_, func, **kw)

    def mm(self, out, lhsT, rhs, **kw):
        self.em("tensor", "matmul", out, lhsT, rhs, **kw)

    def build(self, final_eng="sync"):
        self.bar()
        fin_waits = []
        for tag, need in self.barv.items():
            if need > self.last_wait.get((final_eng, tag), 0):
                fin_waits.append((tag, need))
        self.ops[final_eng].append((None, fin_waits, None, 0))
        sems = self.sem
        with self.nc.Block() as block:
            for eng_name in ENGS:
                entries = self.ops[eng_name]
                if not entries:
                    continue

                def body(eng, entries=entries):
                    for fn, waits, inc_tag, step in entries:
                        for tag, val in waits:
                            eng.wait_ge(sems[tag], val)
                        if fn is not None:
                            inst = fn(eng)
                            if inc_tag is not None:
                                inst.then_inc(sems[inc_tag], step)

                getattr(block, eng_name)(body)
        self.stack.close()
        return self.nc


class _Scope:
    def __init__(self, p):
        self.p = p
        self.stack = contextlib.ExitStack()

    def __enter__(self):
        return self

    def sbuf(self, name, shape, dtype=F32):
        return self.stack.enter_context(self.p.nc.sbuf_tensor(name, shape, dtype))

    def psum(self, name, shape, dtype=F32):
        return self.stack.enter_context(self.p.nc.psum_tensor(name, shape, dtype))

    def __exit__(self, *a):
        self.stack.close()
        return False



class Ctx:
    pass


def ln_tile(p, cx, dst, src, tmp):
    m = tmp[:, 0:1]
    ss = tmp[:, 1:2]
    p.v("tensor_reduce", m, src, AX.X, OP.add)
    p.bar()
    p.v("tensor_scalar", m, m, -1.0 / D, None, OP.mult)
    p.bar()
    p.act(dst, src, AF.Square, bias=m)
    p.bar()
    p.v("tensor_reduce", ss, dst, AX.X, OP.add)
    p.bar()
    np_ = src.shape[0]
    p.act(ss, ss, AF.Sqrt, scale=cx.c_invD[0:np_], bias=cx.c_eps[0:np_])
    p.bar()
    p.v("reciprocal", ss, ss)
    p.bar()
    p.v("tensor_scalar", dst, src, m, ss, OP.add, OP.mult)
    p.bar()


def transpose_tiles(p, cx, dstT, src_tile_fn, tiles=range(NT)):
    for t in tiles:
        src = src_tile_fn(t)
        for half in range(2):
            for j in range(4):
                c = half * 4 + j
                p.t("transpose", cx.tp_ps[:, j * 128:(j + 1) * 128],
                    src[:, c * 128:(c + 1) * 128], cx.eye[:])
            p.bar()
            for j in range(4):
                c = half * 4 + j
                p.s("copy", dstT[:, c, t * 128:(t + 1) * 128], cx.tp_ps[:, j * 128:(j + 1) * 128])
            p.bar()



def host_layout(inputs):
    ins = {k: np.ascontiguousarray(v) for k, v in inputs.items()}
    T = lambda a: np.ascontiguousarray(np.asarray(a).T)

    common = {}
    common["l0_wqkv"] = np.ascontiguousarray(
        np.stack([T(ins["L0_attn_Wq"]), T(ins["L0_attn_Wk"]), T(ins["L0_attn_Wv"]),
                  T(ins["L0_attn_Wo"])], 0))
    common["l0_bqkv"] = np.ascontiguousarray(
        np.stack([ins["L0_attn_bq"], ins["L0_attn_bk"], ins["L0_attn_bv"],
                  ins["L0_attn_bo"]], 0))
    common["l0f_w"] = np.ascontiguousarray(
        np.stack([T(ins["L0_ffn_W_r"]), T(ins["L0_ffn_W_k"]), T(ins["L0_ffn_W_v"])], 0))
    common["l0f_mu"] = np.ascontiguousarray(
        np.stack([ins["L0_ffn_mu_r"], ins["L0_ffn_mu_k"]], 0))
    common["l1_wqkv"] = np.ascontiguousarray(
        np.stack([T(ins["L1_attn_Wq"]), T(ins["L1_attn_Wk"]), T(ins["L1_attn_Wv"]),
                  T(ins["L1_attn_Wo"])], 0))
    common["l1f_wg"] = T(ins["L1_ffn_w_gate"])
    common["l1f_wu"] = T(ins["L1_ffn_w_up"])
    common["l1f_wd"] = T(ins["L1_ffn_w_down"])
    common["l2_wqkv"] = np.ascontiguousarray(
        np.stack([T(ins["L2_attn_Wq"]), T(ins["L2_attn_Wk"]), T(ins["L2_attn_Wv"]),
                  T(ins["L2_attn_Wo"])], 0))
    common["l2f_wgred"] = T(ins["L2_ffn_wg_red"])
    common["l2f_wg"] = ins["L2_ffn_wg"]
    common["l2f_w1"] = ins["L2_ffn_w1"]
    common["l2f_b1"] = ins["L2_ffn_b1"]
    common["l2f_w2"] = ins["L2_ffn_w2"]
    common["l2f_b2"] = ins["L2_ffn_b2"]
    for k in ["lambdas", "k_A", "k_B", "g_A", "g_B", "w_A", "w_B", "a_A", "a_B",
              "x_A", "x_B", "x_time", "gn_gamma", "gn_beta"]:
        common["l3_" + k] = ins["L3_attn_" + k]
    common["l3_misc"] = np.ascontiguousarray(np.stack(
        [ins["L3_attn_w_miu"].reshape(D), ins["L3_attn_a_time"].reshape(D),
         ins["L3_attn_x_miu"].reshape(D)], 0))
    common["l3_wrkvo"] = np.ascontiguousarray(
        np.stack([T(ins["L3_attn_rW"]), T(ins["L3_attn_kW"]), T(ins["L3_attn_vW"]),
                  T(ins["L3_attn_oW"])], 0))
    common["l3f_wg"] = ins["L3_ffn_Wg"]
    common["l3f_b1"] = ins["L3_ffn_b1"]
    common["l3f_b2"] = ins["L3_ffn_b2"]
    common["l3f_w1"] = ins["L3_ffn_W1"]
    common["l3f_w2"] = ins["L3_ffn_W2"]
    dm = np.zeros((16, 128, 1024), np.float32)
    tmod = np.arange(1024) % 16
    for dlt in range(16):
        dm[dlt, :, :] = (tmod >= dlt).astype(np.float32)[None, :]
    common["l3_dmask"] = dm
    bd = np.zeros((128, 128), np.float32)
    bd[0:64, 0:64] = 1.0
    bd[64:128, 64:128] = 1.0
    common["bd64"] = bd
    b2t = np.zeros((2, 128), np.float32)
    b2t[0, 0:64] = 1.0
    b2t[1, 64:128] = 1.0
    common["bd2T"] = np.ascontiguousarray(b2t)

    eye = np.eye(128, dtype=np.float32)

    x = ins["L0_attn_x"]
    maps = []
    for c in range(NCORES):
        m = dict(common)
        xs = x[:, c * PPC:(c + 1) * PPC, :].reshape(TPC, D)
        m["x_in"] = np.ascontiguousarray(xs)
        m["eye"] = eye
        sel = np.zeros((8, 1), np.float32)
        if c > 0:
            sel[c - 1, 0] = 1.0
        m["sel_prev"] = sel
        maps.append(m)
    return maps




def bcast_row(p, cx, psum, row_ap, start=True, stop=True):
    p.mm(psum, cx.ones_row[:, 0:128], row_ap, start=start, stop=stop)


def transpose_tile(p, cx, dstT_c, src, nch=NCH):
    for half in range((nch + 3) // 4):
        cs = list(range(half * 4, min(nch, half * 4 + 4)))
        for j, c in enumerate(cs):
            p.t("transpose", cx.tp_ps[:, j * 128:(j + 1) * 128],
                src[:, c * 128:(c + 1) * 128], cx.eye[:])
        p.bar()
        for j, c in enumerate(cs):
            p.s("copy", dstT_c(c), cx.tp_ps[:, j * 128:(j + 1) * 128])
        p.bar()



def l0_attn(p, cx):
    din = cx.din
    nc = cx.nc
    G = 129
    GS = 256

    kv_ar_in = nc.dram_tensor("l0_kv_in", [B * 2 * 4, 128 * G], F32)
    kv_ar_out = nc.dram_tensor("l0_kv_out", [B * 2 * 4, 128 * G], F32)
    kvin_v = kv_ar_in.ap().rearrange("(b h g) (p c) -> p b h g c", h=2, g=4, p=128)
    kvout_v = kv_ar_out.ap().rearrange("(b h g) (p c) -> b h g p c", h=2, g=4, p=128)

    with p.scope() as osc:
        xT = osc.sbuf("l0_xT", [128, NCH * TPC])
        xTv = xT[:].rearrange("p (c t) -> p c t", c=NCH)

        with p.scope() as sc:
            xln = sc.sbuf("l0_xln", [128, D])
            for t in range(NT):
                ln_tile(p, cx, xln[:], cx.xres_t(t), cx.tmp4[:, 4 * t:4 * t + 4])
                transpose_tile(p, cx, lambda c, t=t: xTv[:, c, t * 128:(t + 1) * 128], xln[:])

        with p.scope() as sc:
            wk = sc.sbuf("l0_wk", [128, NCH * 512])
            wv = sc.sbuf("l0_wv", [128, NCH * 512])
            biases = sc.sbuf("l0_b", [1, 4 * D])
            kk = sc.sbuf("l0_kk", [128, 512])
            vt = sc.sbuf("l0_vt", [128, 4 * G])
            red = sc.sbuf("l0_red", [128, 8])
            kve = sc.sbuf("l0_kve", [128, 4 * G])
            ps_k = sc.psum("l0_ps_k", [128, 512])
            ps_kv = sc.psum("l0_ps_kv", [128, 1024])

            wkv_k = wk[:].rearrange("p (c d) -> p c d", c=NCH)
            wkv_v = wv[:].rearrange("p (c d) -> p c d", c=NCH)
            vt4 = vt[:].rearrange("p (g c) -> p g c", g=4)
            p.dma("sync", biases[:], din["l0_bqkv"].rearrange("a d -> (a d)").unsqueeze(0))
            p.v("memset", vt4[:, :, 128:129], 1.0)
            p.bar()
            for half in range(2):
                p.dma("sync", wkv_k, din["l0_wqkv"][1].rearrange("(c p) d -> p c d", p=128)[:, :, half * 512:(half + 1) * 512])
                p.dma("sync", wkv_v, din["l0_wqkv"][2].rearrange("(c p) d -> p c d", p=128)[:, :, half * 512:(half + 1) * 512])
                p.bar()
                for b in range(B):
                    for tt in range(TPB):
                        t = b * TPB + tt
                        for c in range(NCH):
                            p.mm(ps_k[:], xTv[:, c, t * 128:(t + 1) * 128], wk[:, c * 512:(c + 1) * 512],
                                 start=(c == 0), stop=False)
                        bcast_row(p, cx, ps_k[:], biases[0:1, D + half * 512: D + (half + 1) * 512],
                                  start=False, stop=True)
                        p.bar()
                        k3 = ps_k[:].rearrange("p (h d) -> p h d", d=HD)
                        p.v("tensor_reduce", red[:, 0:8], k3, AX.X, OP.max)
                        p.bar()
                        kk3 = kk[:].rearrange("p (h d) -> p h d", d=HD)
                        p.v("tensor_tensor", kk3, k3, red[:, 0:8].unsqueeze(2).broadcast_to((128, 8, HD)), OP.subtract)
                        p.bar()
                        p.act(kk[:], kk[:], AF.Exp)
                        p.bar()
                        for c in range(NCH):
                            p.mm(ps_k[:], xTv[:, c, t * 128:(t + 1) * 128], wv[:, c * 512:(c + 1) * 512],
                                 start=(c == 0), stop=False)
                        bcast_row(p, cx, ps_k[:], biases[0:1, 2 * D + half * 512: 2 * D + (half + 1) * 512],
                                  start=False, stop=True)
                        p.bar()
                        for g in range(4):
                            p.s("copy", vt4[:, g, 0:128], ps_k[:, g * 128:(g + 1) * 128])
                        p.bar()
                        for g in range(4):
                            p.mm(ps_kv[:, g * GS:g * GS + G], kk[:, g * 128:(g + 1) * 128],
                                 vt4[:, g, :], start=(tt == 0), stop=(tt == TPB - 1))
                        p.bar()
                    for g in range(4):
                        p.s("copy", kve[:, g * G:(g + 1) * G], ps_kv[:, g * GS:g * GS + G])
                    p.bar()
                    p.dma("sync", kvin_v[:, b, half, :, :],
                          kve[:].rearrange("p (g c) -> p g c", g=4))
                    p.bar()
            p.cc("AllReduce", OP.add, [kv_ar_in.ap().opt()], [kv_ar_out.ap().opt()])
            p.bar()

        with p.scope() as sc:
            wq = sc.sbuf("l0_wq", [128, NCH * 512])
            wo = sc.sbuf("l0_wo", [128, 4 * D])
            biases = sc.sbuf("l0_b2", [1, 4 * D])
            kvc = sc.sbuf("l0_kvc", [128, B * 4 * 128])
            ksb = sc.sbuf("l0_ksb", [128, B * 512])
            qk = sc.sbuf("l0_qk", [128, 512])
            qkT = sc.sbuf("l0_qkT", [128, 4 * 128])
            yT = sc.sbuf("l0_yT", [128, 4 * 128])
            dn = sc.sbuf("l0_dn", [128, 512])
            red = sc.sbuf("l0_red2", [128, 16])
            ksum_row = sc.sbuf("l0_ksr", [1, 512])
            ps_a = sc.psum("l0_ps_a", [128, 512])
            ps_b = sc.psum("l0_ps_b", [128, 512])
            ps_t = sc.psum("l0_ps_t", [128, 128])

            p.dma("sync", biases[:], din["l0_bqkv"].rearrange("a d -> (a d)").unsqueeze(0))
            wqv = wq[:].rearrange("p (c d) -> p c d", c=NCH)
            wov = wo[:].rearrange("p (c d) -> p c d", c=4)
            kvcv = kvc[:].rearrange("p (b g c) -> p b g c", b=B, g=4)

            for half in range(2):
                p.dma("sync", wqv, din["l0_wqkv"][0].rearrange("(c p) d -> p c d", p=128)[:, :, half * 512:(half + 1) * 512])
                p.dma("sync", wov, din["l0_wqkv"][3].rearrange("(c p) d -> p c d", p=128)[:, half * 4:half * 4 + 4, :])
                p.v("memset", kvc[:], 0.0)
                p.bar()
                for b in range(B):
                    p.dma("sync", kvcv[0:64, b, :, 0:64],
                          kvout_v[b, half, :, 0:64, 0:64].transpose([1, 0, 2]))
                    p.dma("sync", kvcv[64:128, b, :, 64:128],
                          kvout_v[b, half, :, 64:128, 64:128].transpose([1, 0, 2]))
                    p.dma("sync", ksum_row[0:1, :].rearrange("o (g q) -> o g q", g=4),
                          kvout_v[b, half, :, :, 128].unsqueeze(0), allow_slow_non_contiguous=True)
                    p.bar()
                    bcast_row(p, cx, ps_a[:], ksum_row[0:1, :])
                    p.bar()
                    p.s("copy", ksb[:, b * 512:(b + 1) * 512], ps_a[:])
                    p.bar()
                for b in range(B):
                    for tt in range(TPB):
                        t = b * TPB + tt
                        for c in range(NCH):
                            p.mm(ps_a[:], xTv[:, c, t * 128:(t + 1) * 128], wq[:, c * 512:(c + 1) * 512],
                                 start=(c == 0), stop=False)
                        bcast_row(p, cx, ps_a[:], biases[0:1, half * 512:(half + 1) * 512],
                                  start=False, stop=True)
                        p.bar()
                        q3 = ps_a[:].rearrange("p (h d) -> p h d", d=HD)
                        p.v("tensor_reduce", red[:, 0:8], q3, AX.X, OP.max)
                        p.bar()
                        qk3 = qk[:].rearrange("p (h d) -> p h d", d=HD)
                        p.v("tensor_tensor", qk3, q3, red[:, 0:8].unsqueeze(2).broadcast_to((128, 8, HD)), OP.subtract)
                        p.bar()
                        p.act(qk[:], qk[:], AF.Exp)
                        p.bar()
                        p.v("tensor_tensor", dn[:], qk[:], ksb[:, b * 512:(b + 1) * 512], OP.mult)
                        p.bar()
                        p.v("tensor_reduce", red[:, 8:16], dn[:].rearrange("p (h d) -> p h d", d=HD), AX.X, OP.add)
                        p.bar()
                        p.v("tensor_scalar", red[:, 8:16], red[:, 8:16], 1e-6, None, OP.add)
                        p.bar()
                        p.v("reciprocal", red[:, 8:16], red[:, 8:16])
                        p.bar()
                        p.v("tensor_tensor", qk3, qk3, red[:, 8:16].unsqueeze(2).broadcast_to((128, 8, HD)), OP.mult)
                        p.bar()
                        transpose_tile(p, cx, lambda c: qkT[:, c * 128:(c + 1) * 128], qk[:], nch=4)
                        for g in range(4):
                            p.mm(ps_t[:], kvcv[:, b, g, 0:128], qkT[:, g * 128:(g + 1) * 128],
                                 start=True, stop=True)
                            p.bar()
                            p.s("copy", yT[:, g * 128:(g + 1) * 128], ps_t[:])
                            p.bar()
                        for c in range(4):
                            last = (c == 3) and (half == 0)
                            p.mm(ps_a[:], yT[:, c * 128:(c + 1) * 128], wov[:, c, 0:512],
                                 start=(c == 0), stop=last)
                            p.mm(ps_b[:], yT[:, c * 128:(c + 1) * 128], wov[:, c, 512:1024],
                                 start=(c == 0), stop=last)
                        if half == 1:
                            bcast_row(p, cx, ps_a[:], biases[0:1, 3 * D:3 * D + 512], start=False, stop=True)
                            bcast_row(p, cx, ps_b[:], biases[0:1, 3 * D + 512:4 * D], start=False, stop=True)
                        p.bar()
                        xr = cx.xres_t(t)
                        p.v("tensor_add", xr[:, 0:512], xr[:, 0:512], ps_a[:])
                        p.bar()
                        p.v("tensor_add", xr[:, 512:1024], xr[:, 512:1024], ps_b[:])
                        p.bar()



def l0_ffn(p, cx):
    din = cx.din
    nc = cx.nc

    bnd_in = nc.dram_tensor("l0f_bnd_in", [B, D], F32)
    bnd_out = nc.dram_tensor("l0f_bnd_out", [NCORES * B, D], F32)

    with p.scope() as sc:
        wr = sc.sbuf("l0f_wr", [128, NCH * D])
        wkb = sc.sbuf("l0f_wk", [128, NCH * D])
        wvb = sc.sbuf("l0f_wv", [128, NCH * D])
        mur = sc.sbuf("l0f_mur", [128, D])
        muk = sc.sbuf("l0f_muk", [128, D])
        bnd_all = sc.sbuf("l0f_bndall", [8, D])
        bnd = sc.sbuf("l0f_bnd", [B, D])
        selrow = sc.sbuf("l0f_sel", [8, 1])
        lntmp = sc.sbuf("l0f_lntmp", [B, 4])
        xln = [sc.sbuf("l0f_xln0", [128, D]), sc.sbuf("l0f_xln1", [128, D])]
        xprev = sc.sbuf("l0f_xprev", [128, D])
        mt = sc.sbuf("l0f_mt", [128, D])
        mtT = sc.sbuf("l0f_mtT", [128, NCH * 128])
        rt = sc.sbuf("l0f_rt", [128, D])
        kat = sc.sbuf("l0f_kat", [128, D])
        ps_a = sc.psum("l0f_ps_a", [128, 512])
        ps_b = sc.psum("l0f_ps_b", [128, 512])

        wrv = wr[:].rearrange("p (c d) -> p c d", c=NCH)
        wkv = wkb[:].rearrange("p (c d) -> p c d", c=NCH)
        wvv = wvb[:].rearrange("p (c d) -> p c d", c=NCH)
        mtTv = lambda c: mtT[:, c * 128:(c + 1) * 128]

        p.dma("sync", wrv, din["l0f_w"][0].rearrange("(c p) d -> p c d", p=128))
        p.dma("sync", wkv, din["l0f_w"][1].rearrange("(c p) d -> p c d", p=128))
        p.dma("sync", wvv, din["l0f_w"][2].rearrange("(c p) d -> p c d", p=128))
        p.dma("sync", selrow[:], din["sel_prev"])
        p.bar()
        p.dma("sync", kat[0:1, :], din["l0f_mu"][0:1, :])
        p.dma("sync", mt[0:1, :], din["l0f_mu"][1:2, :])
        p.bar()
        bcast_row(p, cx, ps_a[:], kat[0:1, 0:512])
        bcast_row(p, cx, ps_b[:], kat[0:1, 512:1024])
        p.bar()
        p.s("copy", mur[:, 0:512], ps_a[:])
        p.s("copy", mur[:, 512:1024], ps_b[:])
        p.bar()
        bcast_row(p, cx, ps_a[:], mt[0:1, 0:512])
        bcast_row(p, cx, ps_b[:], mt[0:1, 512:1024])
        p.bar()
        p.s("copy", muk[:, 0:512], ps_a[:])
        p.s("copy", muk[:, 512:1024], ps_b[:])
        p.bar()

        for b in range(B):
            p.dma("sync", kat[b:b + 1, :], cx.xres_t(b * TPB + TPB - 1)[127:128, :])
        p.bar()
        ln_tile(p, cx, bnd[:], kat[0:B, :], lntmp[:])
        p.dma("sync", bnd_in.ap(), bnd[:])
        p.bar()
        p.cc("AllGather", OP.bypass, [bnd_in.ap().opt()], [bnd_out.ap().opt()])
        p.bar()
        for b in range(B):
            p.dma("sync", bnd_all[:], bnd_out.ap().rearrange("(n b) d -> n b d", b=B)[:, b, :])
            p.bar()
            for h2 in range(2):
                p.mm(ps_a[0:1, :], selrow[:], bnd_all[:, h2 * 512:(h2 + 1) * 512],
                     start=True, stop=True)
                p.bar()
                p.s("copy", kat[0:1, h2 * 512:(h2 + 1) * 512], ps_a[0:1, :])
                p.bar()
            p.dma("sync", bnd[b:b + 1, :], kat[0:1, :])
            p.bar()

        for b in range(B):
            for tt in range(TPB):
                t = b * TPB + tt
                cur, prev = xln[t % 2], xln[(t + 1) % 2]
                ln_tile(p, cx, cur[:], cx.xres_t(t), cx.tmp4[:, 4 * t:4 * t + 4])
                p.dma("sync", xprev[1:128, :], cur[0:127, :])
                if tt == 0:
                    p.dma("sync", xprev[0:1, :], bnd[b:b + 1, :])
                else:
                    p.dma("sync", xprev[0:1, :], prev[127:128, :])
                p.bar()
                p.v("tensor_sub", mt[:], cur[:], xprev[:])
                p.bar()
                p.v("tensor_mul", mt[:], mt[:], mur[:])
                p.bar()
                p.v("tensor_add", mt[:], mt[:], xprev[:])
                p.bar()
                transpose_tile(p, cx, mtTv, mt[:])
                for c in range(NCH):
                    p.mm(ps_a[:], mtTv(c), wr[:, c * D:c * D + 512], start=(c == 0), stop=(c == NCH - 1))
                    p.mm(ps_b[:], mtTv(c), wr[:, c * D + 512:(c + 1) * D], start=(c == 0), stop=(c == NCH - 1))
                p.bar()
                p.act(rt[:, 0:512], ps_a[:], AF.Sigmoid)
                p.act(rt[:, 512:1024], ps_b[:], AF.Sigmoid)
                p.bar()
                p.v("tensor_sub", mt[:], cur[:], xprev[:])
                p.bar()
                p.v("tensor_mul", mt[:], mt[:], muk[:])
                p.bar()
                p.v("tensor_add", mt[:], mt[:], xprev[:])
                p.bar()
                transpose_tile(p, cx, mtTv, mt[:])
                for c in range(NCH):
                    p.mm(ps_a[:], mtTv(c), wkb[:, c * D:c * D + 512], start=(c == 0), stop=(c == NCH - 1))
                    p.mm(ps_b[:], mtTv(c), wkb[:, c * D + 512:(c + 1) * D], start=(c == 0), stop=(c == NCH - 1))
                p.bar()
                p.v("tensor_scalar_max", kat[:, 0:512], ps_a[:], 0.0)
                p.v("tensor_scalar_max", kat[:, 512:1024], ps_b[:], 0.0)
                p.bar()
                p.act(kat[:], kat[:], AF.Square)
                p.bar()
                transpose_tile(p, cx, mtTv, kat[:])
                for c in range(NCH):
                    p.mm(ps_a[:], mtTv(c), wvb[:, c * D:c * D + 512], start=(c == 0), stop=(c == NCH - 1))
                    p.mm(ps_b[:], mtTv(c), wvb[:, c * D + 512:(c + 1) * D], start=(c == 0), stop=(c == NCH - 1))
                p.bar()
                p.v("tensor_mul", rt[:, 0:512], rt[:, 0:512], ps_a[:])
                p.bar()
                p.v("tensor_mul", rt[:, 512:1024], rt[:, 512:1024], ps_b[:])
                p.bar()
                xr = cx.xres_t(t)
                p.v("tensor_add", xr[:], xr[:], rt[:])
                p.bar()




def round_clip(p, cx, dst, src, iscr, yscr, lo, hi, pre_scale=None):
    if pre_scale is not None:
        p.v("tensor_scalar", yscr, src, pre_scale, 0.5, OP.mult, OP.add)
    else:
        p.v("tensor_scalar", yscr, src, 0.5, None, OP.add)
    p.bar()
    p.v("tensor_copy", iscr, yscr)
    p.bar()
    p.v("tensor_copy", dst, iscr)
    p.bar()
    p.v("tensor_tensor", iscr.bitcast(F32), dst, yscr, OP.is_gt)
    p.bar()
    p.v("tensor_tensor", dst, dst, iscr.bitcast(F32), OP.subtract)
    p.bar()
    p.v("tensor_scalar", dst, dst, float(lo), float(hi), OP.max, OP.min)
    p.bar()


def part_max_scalar(p, cx, dst11, col128, ps128):
    p.t("transpose", ps128[0:1, 0:128], col128, cx.eye[:])
    p.bar()
    p.v("tensor_reduce", dst11, ps128[0:1, 0:128], AX.X, OP.max)
    p.bar()


def bcast_scalar(p, cx, dst_col, src11, ps):
    p.mm(ps[:, 0:1], cx.ones_row[:, 0:128], src11, start=True, stop=True)
    p.bar()
    p.s("copy", dst_col, ps[:, 0:1])
    p.bar()


class ARBox:
    def __init__(self, p, sc, nc, name):
        self.buf = sc.sbuf(name + "_b", [1, 16])
        self.d_in = nc.dram_tensor(name + "_i", [1, 16], F32)
        self.d_out = nc.dram_tensor(name + "_o", [1, 16], F32)

    def ar_max(self, p):
        p.dma("sync", self.d_in.ap(), self.buf[:])
        p.bar()
        p.cc("AllReduce", OP.max, [self.d_in.ap().opt()], [self.d_out.ap().opt()])
        p.bar()
        p.dma("sync", self.buf[:], self.d_out.ap())
        p.bar()



def l1_attn(p, cx):
    din = cx.din
    nc = cx.nc
    xres_dram = nc.dram_tensor("l1_xres_spill", [128, NT * D], F32)

    xr_bf = cx.x_res[:].bitcast(BF16)
    qst = xr_bf[:, 0:NT * D]
    kst = xr_bf[:, NT * D:2 * NT * D]

    with p.scope() as osc:
        vst = osc.sbuf("l1_v", [128, NT * D], BF16)
        scal = osc.sbuf("l1_scal", [128, 32])
        s11 = osc.sbuf("l1_s11", [1, 16])
        arb = ARBox(p, osc, nc, "l1_arb")
        attn = osc.sbuf("l1_attn", [128, TPB * 256])
        ps_m = osc.psum("l1_ps_m", [128, 128])

        (SX, FQ, FK, FV, SQ, SK, SV, SA, SO, SOUT, T0, T1, T2, T3, MQ) = range(15)
        col = lambda i: scal[:, i:i + 1]

        def scalar_from_col(idx, dst11=None):
            part_max_scalar(p, cx, s11[0:1, 0:1] if dst11 is None else dst11, col(idx), ps_m)

        with p.scope() as sc2:
            xqT = sc2.sbuf("l1_xqT", [128, NCH * TPC], BF16)
            xln = sc2.sbuf("l1_xln", [128, D])
            iscr = sc2.sbuf("l1_iscr", [128, D], I32)
            fscr = sc2.sbuf("l1_fscr", [128, D])
            amax = sc2.sbuf("l1_amax", [128, 1])
            xqTv = xqT[:].rearrange("p (c t) -> p c t", c=NCH)

            p.v("memset", amax[:], 0.0)
            p.bar()
            for t in range(NT):
                ln_tile(p, cx, xln[:], cx.xres_t(t), cx.tmp4[:, 4 * t:4 * t + 4])
                p.v("tensor_reduce", col(T0), xln[:], AX.X, OP.max, apply_absolute_value=True)
                p.bar()
                p.v("tensor_max", amax[:], amax[:], col(T0))
                p.bar()
                if cx.dbg_f is not None:
                    part_max_scalar(p, cx, s11[0:1, 8:9], col(T0), ps_m)
                    p.dma("sync", cx.dbg_f[0:1, 1120 + t:1121 + t], s11[0:1, 8:9])
                    p.bar()
                if cx.dbg_f is not None and t == 0:
                    p.dma("sync", cx.dbg_f[:, 0:1024], xln[:])
                    p.bar()
            part_max_scalar(p, cx, s11[0:1, 0:1], amax[:], ps_m)
            p.v("memset", arb.buf[:], 0.0)
            p.bar()
            p.s("copy", arb.buf[0:1, 0:1], s11[0:1, 0:1])
            p.bar()
            arb.ar_max(p)
            bcast_scalar(p, cx, col(SX), arb.buf[0:1, 0:1], ps_m)
            p.v("tensor_scalar", col(SX), col(SX), 1.0 / 127.0, 1e-8, OP.mult, OP.max)
            p.bar()
            p.v("reciprocal", col(T1), col(SX))
            p.bar()
            for t in range(NT):
                ln_tile(p, cx, xln[:], cx.xres_t(t), cx.tmp4[:, 4 * t:4 * t + 4])
                round_clip(p, cx, xln[:], xln[:], iscr[:], fscr[:], -128, 127, pre_scale=col(T1))
                for half in range(2):
                    for j in range(4):
                        c = half * 4 + j
                        p.t("transpose", cx.tp_ps[:, j * 128:(j + 1) * 128],
                            xln[:, c * 128:(c + 1) * 128], cx.eye[:])
                    p.bar()
                    for j in range(4):
                        c = half * 4 + j
                        p.s("copy", xqTv[:, c, t * 128:(t + 1) * 128], cx.tp_ps[:, j * 128:(j + 1) * 128])
                    p.bar()
            p.dma("sync", xres_dram.ap(), cx.x_res[:])
            p.bar()

            with p.scope() as sc3:
                wfp = sc3.sbuf("l1_wfp", [128, NCH * 512])
                wint = sc3.sbuf("l1_wint", [128, NCH * D], BF16)
                iscr2 = sc3.sbuf("l1_iscr2", [128, 512], I32)
                fscr2 = sc3.sbuf("l1_fscr2", [128, 512])
                fscr2b = sc3.sbuf("l1_fscr2b", [128, 512])
                wmax = sc3.sbuf("l1_wmax", [128, 1])
                ps_p = sc3.psum("l1_ps_p", [128, 512])
                wintv = wint[:].rearrange("p (c d) -> p c d", c=NCH)
                wfpv = wfp[:].rearrange("p (c d) -> p c d", c=NCH)

                def quant_weight(wi, sw_col):
                    p.v("memset", wmax[:], 0.0)
                    p.bar()
                    for half in range(2):
                        p.dma("sync", wfpv, din["l1_wqkv"][wi].rearrange("(c p) d -> p c d", p=128)[:, :, half * 512:(half + 1) * 512])
                        p.bar()
                        p.v("tensor_reduce", col(T0), wfp[:], AX.X, OP.max, apply_absolute_value=True)
                        p.bar()
                        p.v("tensor_max", wmax[:], wmax[:], col(T0))
                        p.bar()
                        part_max_scalar(p, cx, s11[0:1, 4:5], wmax[:], ps_m) if half == 1 else None
                    bcast_scalar(p, cx, col(sw_col), s11[0:1, 4:5], ps_m)
                    p.v("tensor_scalar", col(sw_col), col(sw_col), 1.0 / 127.0, 1e-8, OP.mult, OP.max)
                    p.bar()
                    p.v("reciprocal", col(T1), col(sw_col))
                    p.bar()
                    for half in range(2):
                        p.dma("sync", wfpv, din["l1_wqkv"][wi].rearrange("(c p) d -> p c d", p=128)[:, :, half * 512:(half + 1) * 512])
                        p.bar()
                        for c in range(NCH):
                            round_clip(p, cx, fscr2b[:], wfpv[:, c, :], iscr2[:], fscr2[:], -127, 127, pre_scale=col(T1))
                            p.v("tensor_copy", wintv[:, c, half * 512:(half + 1) * 512], fscr2b[:])
                            p.bar()

                def proj_pass(store, maxcol, factor_col=None):
                    if not hasattr(proj_pass, "first"):
                        proj_pass.first = [True]
                    """If factor_col None: accumulate abs-max of int matmul into col(maxcol).
                    Else: recompute, quantize with factor, write int grid into store."""
                    if factor_col is None:
                        p.v("memset", col(maxcol), 0.0)
                        p.bar()
                    for t in range(NT):
                        for dh in range(2):
                            for c in range(NCH):
                                p.mm(ps_p[:], xqTv[:, c, t * 128:(t + 1) * 128],
                                     wintv[:, c, dh * 512:(dh + 1) * 512],
                                     start=(c == 0), stop=(c == NCH - 1))
                            p.bar()
                            if factor_col is None:
                                p.v("tensor_reduce", col(T0), ps_p[:], AX.X, OP.max, apply_absolute_value=True)
                                p.bar()
                                p.v("tensor_max", col(maxcol), col(maxcol), col(T0))
                                p.bar()
                                if cx.dbg_f is not None and t == 0 and dh == 0 and proj_pass.first[0]:
                                    proj_pass.first[0] = False
                                    p.dma("sync", cx.dbg_f[:, 1107:1108], col(T0), allow_slow_non_contiguous=True)
                                    p.dma("sync", cx.dbg_f[:, 1108:1109], col(maxcol), allow_slow_non_contiguous=True)
                                    p.bar()
                            else:
                                round_clip(p, cx, fscr2b[:], ps_p[:], iscr2[:], fscr2[:], -128, 127,
                                           pre_scale=col(factor_col))
                                p.v("tensor_copy", store[:, t * D + dh * 512:t * D + (dh + 1) * 512], fscr2b[:])
                                p.bar()

                stores = [qst, kst, vst[:]]
                fqcols = [FQ, FK, FV]
                sqcols = [SQ, SK, SV]
                premul = [1.0 / 8.0, 1.0, 1.0]
                for wi in range(3):
                    quant_weight(wi, T2)
                    if wi == 0 and cx.dbg_f is not None:
                        p.v("tensor_copy", fscr2[:], wintv[:, 0, 0:512])
                        p.bar()
                        p.dma("sync", cx.dbg_f[:, 1152:1664], fscr2[:])
                        p.bar()
                        for c in range(NCH):
                            p.mm(ps_p[:], xqTv[:, c, 0:128], wintv[:, c, 0:512],
                                 start=(c == 0), stop=(c == NCH - 1))
                        p.bar()
                        p.v("tensor_copy", fscr2[:], ps_p[:])
                        p.bar()
                        p.dma("sync", cx.dbg_f[:, 1664:2048], fscr2[:, 0:384])
                        p.bar()
                    proj_pass(None, MQ)
                    p.v("tensor_mul", col(fqcols[wi]), col(SX), col(T2))
                    p.bar()
                    p.v("tensor_scalar", col(fqcols[wi]), col(fqcols[wi]), premul[wi], None, OP.mult)
                    p.bar()
                    p.v("tensor_mul", col(T3), col(MQ), col(fqcols[wi]))
                    p.bar()
                    part_max_scalar(p, cx, s11[0:1, wi:wi + 1], col(T3), ps_m)
                if cx.dbg_f is not None:
                    p.dma("sync", cx.dbg_f[0:1, 1100:1103], s11[0:1, 0:3])
                    p.dma("sync", cx.dbg_f[:, 1103:1104], col(MQ), allow_slow_non_contiguous=True)
                    p.bar()
                p.v("memset", arb.buf[:], 0.0)
                p.bar()
                p.s("copy", arb.buf[0:1, 0:3], s11[0:1, 0:3])
                p.bar()
                arb.ar_max(p)
                if cx.dbg_f is not None:
                    p.dma("sync", cx.dbg_f[0:1, 1104:1107], arb.buf[0:1, 0:3])
                    p.bar()
                for wi in range(3):
                    bcast_scalar(p, cx, col(sqcols[wi]), arb.buf[0:1, wi:wi + 1], ps_m)
                    p.v("tensor_scalar", col(sqcols[wi]), col(sqcols[wi]), 1.0 / 127.0, 1e-8, OP.mult, OP.max)
                    p.bar()
                for wi in range(3):
                    quant_weight(wi, T2)
                    p.v("reciprocal", col(T3), col(sqcols[wi]))
                    p.bar()
                    p.v("tensor_mul", col(T3), col(T3), col(fqcols[wi]))
                    p.bar()
                    proj_pass(stores[wi], None, factor_col=T3)


        with p.scope() as sc4:
            scr = sc4.sbuf("l1_scr", [128, 256])
            mx = sc4.sbuf("l1_mx", [128, 64])
            sm = sc4.sbuf("l1_sm", [128, 64])
            mulbuf = sc4.sbuf("l1_mul", [128, D])
            amax2 = sc4.sbuf("l1_amax2", [128, 1])

            qv = lambda t: qst[:, t * D:(t + 1) * D]
            kv_ = lambda t: kst[:, t * D:(t + 1) * D]
            vv = lambda t: vst[:, t * D:(t + 1) * D]
            scr3 = scr[:].rearrange("p (a h j) -> p a h j", a=B, h=NH)

            p.v("memset", amax2[:], 0.0)
            p.bar()
            p.v("tensor_mul", col(T2), col(SQ), col(SK))
            p.bar()
            for tt in range(TPB):
                for si in range(B):
                    for sj in range(B):
                        p.v("tensor_mul", mulbuf[:], qv(si * TPB + tt), kv_(sj * TPB + tt))
                        p.bar()
                        p.v("tensor_reduce", scr3[:, si, :, sj],
                            mulbuf[:].rearrange("p (h d) -> p h d", d=HD), AX.X, OP.add)
                        p.bar()
                p.v("tensor_scalar", scr[:], scr[:], col(T2), None, OP.mult)
                p.bar()
                p.v("tensor_reduce", mx[:], scr[:].rearrange("p (m j) -> p m j", j=B), AX.X, OP.max)
                p.bar()
                p.v("tensor_tensor", scr[:].rearrange("p (m j) -> p m j", j=B),
                    scr[:].rearrange("p (m j) -> p m j", j=B),
                    mx[:].unsqueeze(2).broadcast_to((128, 64, B)), OP.subtract)
                p.bar()
                p.act(scr[:], scr[:], AF.Exp)
                p.bar()
                p.v("tensor_reduce", sm[:], scr[:].rearrange("p (m j) -> p m j", j=B), AX.X, OP.add)
                p.bar()
                p.v("reciprocal", sm[:], sm[:])
                p.bar()
                p.v("tensor_tensor", scr[:].rearrange("p (m j) -> p m j", j=B),
                    scr[:].rearrange("p (m j) -> p m j", j=B),
                    sm[:].unsqueeze(2).broadcast_to((128, 64, B)), OP.mult)
                p.bar()
                p.s("copy", attn[:, tt * 256:(tt + 1) * 256], scr[:])
                p.bar()
                p.v("tensor_reduce", col(T0), scr[:], AX.X, OP.max)
                p.bar()
                p.v("tensor_max", amax2[:], amax2[:], col(T0))
                p.bar()
            part_max_scalar(p, cx, s11[0:1, 0:1], amax2[:], ps_m)
            p.v("memset", arb.buf[:], 0.0)
            p.bar()
            p.s("copy", arb.buf[0:1, 0:1], s11[0:1, 0:1])
            p.bar()
            arb.ar_max(p)
            bcast_scalar(p, cx, col(SA), arb.buf[0:1, 0:1], ps_m)
            p.v("tensor_scalar", col(SA), col(SA), 1.0 / 255.0, 1e-8, OP.mult, OP.max)
            p.bar()
            p.v("reciprocal", col(T1), col(SA))
            p.bar()

            with p.scope() as sc5:
                aq = sc5.sbuf("l1_aq", [128, 256])
                iscr3 = sc5.sbuf("l1_iscr3", [128, 256], I32)
                fscr3 = sc5.sbuf("l1_fscr3", [128, 256])
                ot = sc5.sbuf("l1_ot", [128, D])
                oq = sc5.sbuf("l1_oq", [128, D])
                iscr4 = sc5.sbuf("l1_iscr4", [128, D], I32)
                fscr4 = sc5.sbuf("l1_fscr4", [128, D])

                def o_tile(tt, si, aq3):
                    ot3 = ot[:].rearrange("p (h d) -> p h d", d=HD)
                    mb3 = mulbuf[:].rearrange("p (h d) -> p h d", d=HD)
                    for sj in range(B):
                        ab = aq3[:, si, :, sj].unsqueeze(2).broadcast_to((128, NH, HD))
                        v3 = vv(sj * TPB + tt).rearrange("p (h d) -> p h d", d=HD)
                        if sj == 0:
                            p.v("tensor_tensor", ot3, v3, ab, OP.mult)
                        else:
                            p.v("tensor_tensor", mb3, v3, ab, OP.mult)
                            p.bar()
                            p.v("tensor_add", ot[:], ot[:], mulbuf[:])
                        p.bar()

                aq3 = aq[:].rearrange("p (a h j) -> p a h j", a=B, h=NH)
                p.v("memset", amax2[:], 0.0)
                p.bar()
                p.v("tensor_mul", col(T2), col(SA), col(SV))
                p.bar()
                for tt in range(TPB):
                    round_clip(p, cx, aq[:], attn[:, tt * 256:(tt + 1) * 256], iscr3[:], fscr3[:],
                               0, 255, pre_scale=col(T1))
                    for si in range(B):
                        o_tile(tt, si, aq3)
                        p.v("tensor_reduce", col(T0), ot[:], AX.X, OP.max, apply_absolute_value=True)
                        p.bar()
                        p.v("tensor_max", amax2[:], amax2[:], col(T0))
                        p.bar()
                p.v("tensor_mul", amax2[:], amax2[:], col(T2))
                p.bar()
                part_max_scalar(p, cx, s11[0:1, 0:1], amax2[:], ps_m)
                p.v("memset", arb.buf[:], 0.0)
                p.bar()
                p.s("copy", arb.buf[0:1, 0:1], s11[0:1, 0:1])
                p.bar()
                arb.ar_max(p)
                bcast_scalar(p, cx, col(SO), arb.buf[0:1, 0:1], ps_m)
                p.v("tensor_scalar", col(SO), col(SO), 1.0 / 127.0, 1e-8, OP.mult, OP.max)
                p.bar()
                p.v("reciprocal", col(T3), col(SO))
                p.bar()
                p.v("tensor_mul", col(T3), col(T3), col(T2))
                p.bar()
                for tt in range(TPB):
                    round_clip(p, cx, aq[:], attn[:, tt * 256:(tt + 1) * 256], iscr3[:], fscr3[:],
                               0, 255, pre_scale=col(T1))
                    for si in range(B):
                        o_tile(tt, si, aq3)
                        round_clip(p, cx, oq[:], ot[:], iscr4[:], fscr4[:], -128, 127, pre_scale=col(T3))
                        p.v("tensor_copy", qv(si * TPB + tt), oq[:])
                        p.bar()

        with p.scope() as sc6:
            wfp = sc6.sbuf("l1d_wfp", [128, NCH * 512])
            wint = sc6.sbuf("l1d_wint", [128, NCH * D], BF16)
            iscr2 = sc6.sbuf("l1d_iscr2", [128, 512], I32)
            fscr2 = sc6.sbuf("l1d_fscr2", [128, 512])
            fscr2b = sc6.sbuf("l1d_fscr2b", [128, 512])
            wmax = sc6.sbuf("l1d_wmax", [128, 1])
            oqT = sc6.sbuf("l1d_oqT", [128, NCH * 128], BF16)
            ps_p = sc6.psum("l1d_ps_p", [128, 512])
            wintv = wint[:].rearrange("p (c d) -> p c d", c=NCH)
            wfpv = wfp[:].rearrange("p (c d) -> p c d", c=NCH)
            oqTv = lambda c: oqT[:, c * 128:(c + 1) * 128]

            p.v("memset", wmax[:], 0.0)
            p.bar()
            for half in range(2):
                p.dma("sync", wfpv, din["l1_wqkv"][3].rearrange("(c p) d -> p c d", p=128)[:, :, half * 512:(half + 1) * 512])
                p.bar()
                p.v("tensor_reduce", col(T0), wfp[:], AX.X, OP.max, apply_absolute_value=True)
                p.bar()
                p.v("tensor_max", wmax[:], wmax[:], col(T0))
                p.bar()
            part_max_scalar(p, cx, s11[0:1, 0:1], wmax[:], ps_m)
            bcast_scalar(p, cx, col(T2), s11[0:1, 0:1], ps_m)
            p.v("tensor_scalar", col(T2), col(T2), 1.0 / 127.0, 1e-8, OP.mult, OP.max)
            p.bar()
            p.v("reciprocal", col(T1), col(T2))
            p.bar()
            for half in range(2):
                p.dma("sync", wfpv, din["l1_wqkv"][3].rearrange("(c p) d -> p c d", p=128)[:, :, half * 512:(half + 1) * 512])
                p.bar()
                for c in range(NCH):
                    round_clip(p, cx, fscr2b[:], wfpv[:, c, :], iscr2[:], fscr2[:], -127, 127, pre_scale=col(T1))
                    p.v("tensor_copy", wintv[:, c, half * 512:(half + 1) * 512], fscr2b[:])
                    p.bar()
            p.v("tensor_mul", col(T2), col(T2), col(SO))
            p.bar()
            p.v("memset", wmax[:], 0.0)
            p.bar()

            def out_mms(t):
                for half in range(2):
                    for j in range(4):
                        c = half * 4 + j
                        p.s("copy", fscr2[:, 0:128], qv(t)[:, c * 128:(c + 1) * 128])
                        p.bar()
                        p.t("transpose", cx.tp_ps[:, j * 128:(j + 1) * 128], fscr2[:, 0:128], cx.eye[:])
                        p.bar()
                    for j in range(4):
                        c = half * 4 + j
                        p.s("copy", oqTv(c), cx.tp_ps[:, j * 128:(j + 1) * 128])
                    p.bar()

            for t in range(NT):
                out_mms(t)
                for dh in range(2):
                    for c in range(NCH):
                        p.mm(ps_p[:], oqTv(c), wintv[:, c, dh * 512:(dh + 1) * 512],
                             start=(c == 0), stop=(c == NCH - 1))
                    p.bar()
                    p.v("tensor_reduce", col(T0), ps_p[:], AX.X, OP.max, apply_absolute_value=True)
                    p.bar()
                    p.v("tensor_max", wmax[:], wmax[:], col(T0))
                    p.bar()
            p.v("tensor_mul", wmax[:], wmax[:], col(T2))
            p.bar()
            part_max_scalar(p, cx, s11[0:1, 0:1], wmax[:], ps_m)
            p.v("memset", arb.buf[:], 0.0)
            p.bar()
            p.s("copy", arb.buf[0:1, 0:1], s11[0:1, 0:1])
            p.bar()
            arb.ar_max(p)
            bcast_scalar(p, cx, col(SOUT), arb.buf[0:1, 0:1], ps_m)
            p.v("tensor_scalar", col(SOUT), col(SOUT), 1.0 / 127.0, 1e-8, OP.mult, OP.max)
            p.bar()
            p.v("reciprocal", col(T1), col(SOUT))
            p.bar()
            p.v("tensor_mul", col(T1), col(T1), col(T2))
            p.bar()
            for t in range(NT):
                out_mms(t)
                for dh in range(2):
                    for c in range(NCH):
                        p.mm(ps_p[:], oqTv(c), wintv[:, c, dh * 512:(dh + 1) * 512],
                             start=(c == 0), stop=(c == NCH - 1))
                    p.bar()
                    round_clip(p, cx, fscr2b[:], ps_p[:], iscr2[:], fscr2[:], -128, 127, pre_scale=col(T1))
                    p.v("tensor_copy", vst[:, t * D + dh * 512:t * D + (dh + 1) * 512], fscr2b[:])
                    p.bar()
            p.dma("sync", cx.x_res[:], xres_dram.ap())
            p.bar()
            for t in range(NT):
                xr = cx.xres_t(t)
                p.v("tensor_scalar", wfp[:, 0:D], vst[:, t * D:(t + 1) * D], col(SOUT), None, OP.mult)
                p.bar()
                p.v("tensor_add", xr[:], xr[:], wfp[:, 0:D])
                p.bar()


BLOCK_FNS = [l0_attn, l0_ffn, l1_attn]



def l1_ffn(p, cx):
    din = cx.din
    nc = cx.nc
    FF = 4096
    NG = 4
    xres_dram = nc.dram_tensor("l1f_xres_spill", [128, NT * D], F32)
    outst = cx.x_res

    with p.scope() as osc:
        xqT = osc.sbuf("l1f_xqT", [128, NCH * TPC], BF16)
        rx = osc.sbuf("l1f_rx", [128, NT])
        rh = osc.sbuf("l1f_rh", [128, NT])
        hmax = osc.sbuf("l1f_hmax", [128, NT])
        scal = osc.sbuf("l1f_scal", [128, 8])
        s11 = osc.sbuf("l1f_s11", [1, 8])
        ps_m = osc.psum("l1f_ps_m", [128, 128])
        RWG, RWU, RWD, T0, T1, T2, T3, T4 = range(8)
        col = lambda i: scal[:, i:i + 1]
        xqTv = xqT[:].rearrange("p (c t) -> p c t", c=NCH)

        with p.scope() as sc:
            xln = sc.sbuf("l1f_xln", [128, D])
            iscr = sc.sbuf("l1f_iscr", [128, D], I32)
            fscr = sc.sbuf("l1f_fscr", [128, D])
            for t in range(NT):
                ln_tile(p, cx, xln[:], cx.xres_t(t), cx.tmp4[:, 4 * t:4 * t + 4])
                rxc = rx[:, t:t + 1]
                p.v("tensor_reduce", rxc, xln[:], AX.X, OP.max, apply_absolute_value=True)
                p.bar()
                p.v("tensor_scalar", rxc, rxc, 1e-5, 1.0 / 127.0, OP.max, OP.mult)
                p.bar()
                p.v("reciprocal", col(T0), rxc)
                p.bar()
                round_clip(p, cx, xln[:], xln[:], iscr[:], fscr[:], -128, 127, pre_scale=col(T0))
                for half in range(2):
                    for j in range(4):
                        c = half * 4 + j
                        p.t("transpose", cx.tp_ps[:, j * 128:(j + 1) * 128],
                            xln[:, c * 128:(c + 1) * 128], cx.eye[:])
                    p.bar()
                    for j in range(4):
                        c = half * 4 + j
                        p.s("copy", xqTv[:, c, t * 128:(t + 1) * 128], cx.tp_ps[:, j * 128:(j + 1) * 128])
                    p.bar()
            p.dma("sync", xres_dram.ap(), cx.x_res[:])
            p.bar()

        with p.scope() as sc:
            wfp = sc.sbuf("l1f_wfp", [128, 8 * 1024])
            acc = sc.sbuf("l1f_acc", [128, 1])
            ones_col = sc.sbuf("l1f_ones", [128, 1])
            p.v("memset", ones_col[:], 1.0)
            p.bar()

            def mean_abs(dram_ap, n_elem, scol):
                p.v("memset", acc[:], 0.0)
                p.bar()
                n_pass = n_elem // (128 * 8192)
                for i in range(n_pass):
                    p.dma("sync", wfp[:], dram_ap.rearrange("(n p) q -> n p q", p=128)[i])
                    p.bar()
                    p.v("tensor_reduce", col(T0), wfp[:], AX.X, OP.add, apply_absolute_value=True)
                    p.bar()
                    p.v("tensor_add", acc[:], acc[:], col(T0))
                    p.bar()
                p.t("transpose", ps_m[0:1, 0:128], acc[:], cx.eye[:])
                p.bar()
                p.v("tensor_reduce", s11[0:1, 0:1], ps_m[0:1, 0:128], AX.X, OP.add)
                p.bar()
                bcast_scalar(p, cx, col(scol), s11[0:1, 0:1], ps_m)
                p.v("tensor_scalar", col(scol), col(scol), 1.0 / n_elem, 1e-5, OP.mult, OP.max)
                p.bar()

            mean_abs(din["l1f_wg"].rearrange("a b -> (a b)").unsqueeze(0).rearrange("o (n q) -> (o n) q", q=8192), D * FF, RWG)
            mean_abs(din["l1f_wu"].rearrange("a b -> (a b)").unsqueeze(0).rearrange("o (n q) -> (o n) q", q=8192), D * FF, RWU)
            mean_abs(din["l1f_wd"].rearrange("a b -> (a b)").unsqueeze(0).rearrange("o (n q) -> (o n) q", q=8192), FF * D, RWD)

        def quant_w_group(dst_int, dram_ap, g, scol, wfp, iscr2, fscr2, fscr2b, nch_in, lo, hi):
            p.v("reciprocal", col(T4), col(scol))
            p.bar()
            for ci in range(nch_in):
                p.dma("sync", wfp[:, 0:1024],
                      dram_ap[ci * 128:(ci + 1) * 128, g * 1024:(g + 1) * 1024])
                p.bar()
                round_clip(p, cx, fscr2b[:], wfp[:, 0:1024], iscr2[:], fscr2[:], lo, hi, pre_scale=col(T4))
                p.v("tensor_copy", dst_int[:, ci, :], fscr2b[:])
                p.bar()

        for phase in range(2):
            with p.scope() as sc:
                P = f"l1f{phase}_"
                wgi = sc.sbuf(P + "wgi", [128, NCH * 1024], BF16)
                wui = sc.sbuf(P + "wui", [128, NCH * 1024], BF16)
                wdi = sc.sbuf(P + "wdi", [128, NCH * 1024], BF16) if phase == 1 else None
                wfp2 = sc.sbuf(P + "wfp2", [128, 1024])
                iscr2 = sc.sbuf(P + "iscr2", [128, 1024], I32)
                fscr2 = sc.sbuf(P + "fscr2", [128, 1024])
                fscr2b = sc.sbuf(P + "fscr2b", [128, 1024])
                ht = sc.sbuf(P + "ht", [128, 1024])
                gt = sc.sbuf(P + "gt", [128, 1024])
                hq = sc.sbuf(P + "hq", [128, 1024])
                hqT = sc.sbuf(P + "hqT", [128, 1024], BF16)
                ps_a = sc.psum(P + "ps_a", [128, 512])
                ps_b = sc.psum(P + "ps_b", [128, 512])
                wgiv = wgi[:].rearrange("p (c d) -> p c d", c=NCH)
                wuiv = wui[:].rearrange("p (c d) -> p c d", c=NCH)
                if phase == 0:
                    p.v("memset", hmax[:], 0.0)
                    p.bar()
                for g in range(NG):
                    quant_w_group(wgiv, din["l1f_wg"], g, RWG, wfp2, iscr2, fscr2, fscr2b, NCH, -1, 1)
                    quant_w_group(wuiv, din["l1f_wu"], g, RWU, wfp2, iscr2, fscr2, fscr2b, NCH, -1, 1)
                    if phase == 1:
                        wdiv = wdi[:].rearrange("p (c d) -> p c d", c=NCH)
                        quant_w_group(wdiv, din["l1f_wd"][g * 1024:(g + 1) * 1024, :], 0, RWD,
                                      wfp2, iscr2, fscr2, fscr2b, NCH, -1, 1)
                    for t in range(NT):
                        for dh in range(2):
                            for c in range(NCH):
                                p.mm(ps_a[:], xqTv[:, c, t * 128:(t + 1) * 128],
                                     wgiv[:, c, dh * 512:(dh + 1) * 512],
                                     start=(c == 0), stop=(c == NCH - 1))
                                p.mm(ps_b[:], xqTv[:, c, t * 128:(t + 1) * 128],
                                     wuiv[:, c, dh * 512:(dh + 1) * 512],
                                     start=(c == 0), stop=(c == NCH - 1))
                            p.bar()
                            p.v("tensor_mul", col(T0), rx[:, t:t + 1], col(RWG))
                            p.bar()
                            p.v("tensor_mul", col(T1), rx[:, t:t + 1], col(RWU))
                            p.bar()
                            p.act(gt[:, dh * 512:(dh + 1) * 512], ps_a[:], AF.Silu, scale=col(T0))
                            p.bar()
                            p.v("tensor_scalar", ht[:, dh * 512:(dh + 1) * 512], ps_b[:], col(T1), None, OP.mult)
                            p.bar()
                        p.v("tensor_mul", ht[:], ht[:], gt[:])
                        p.bar()
                        if phase == 0:
                            p.v("tensor_reduce", col(T2), ht[:], AX.X, OP.max, apply_absolute_value=True)
                            p.bar()
                            p.v("tensor_max", hmax[:, t:t + 1], hmax[:, t:t + 1], col(T2))
                            p.bar()
                        else:
                            p.v("reciprocal", col(T2), rh[:, t:t + 1])
                            p.bar()
                            round_clip(p, cx, hq[:], ht[:], iscr2[:], fscr2[:], -128, 127, pre_scale=col(T2))
                            for half in range(2):
                                for j in range(4):
                                    c = half * 4 + j
                                    p.t("transpose", cx.tp_ps[:, j * 128:(j + 1) * 128],
                                        hq[:, c * 128:(c + 1) * 128], cx.eye[:])
                                p.bar()
                                for j in range(4):
                                    c = half * 4 + j
                                    p.s("copy", hqT[:, c * 128:(c + 1) * 128], cx.tp_ps[:, j * 128:(j + 1) * 128])
                                p.bar()
                            for dh in range(2):
                                for c in range(NCH):
                                    p.mm(ps_a[:], hqT[:, c * 128:(c + 1) * 128],
                                         wdiv[:, c, dh * 512:(dh + 1) * 512],
                                         start=(c == 0), stop=(c == NCH - 1))
                                p.bar()
                                dst = cx.xres_t(t)[:, dh * 512:(dh + 1) * 512]
                                if g == 0:
                                    p.s("copy", dst, ps_a[:])
                                else:
                                    p.v("tensor_add", dst, dst, ps_a[:])
                                p.bar()
                if phase == 0:
                    for t in range(NT):
                        p.v("tensor_scalar", rh[:, t:t + 1], hmax[:, t:t + 1], 1e-5, 1.0 / 127.0, OP.max, OP.mult)
                    p.bar()

        with p.scope() as sc:
            xrt = sc.sbuf("l1f_xrt", [128, D])
            for t in range(NT):
                p.v("tensor_mul", col(T0), rh[:, t:t + 1], col(RWD))
                p.bar()
                p.dma("sync", xrt[:], xres_dram.ap().rearrange("p (t f) -> p t f", t=NT)[:, t, :])
                p.bar()
                xr = cx.xres_t(t)
                p.v("tensor_scalar", xr[:], xr[:], col(T0), None, OP.mult)
                p.bar()
                p.v("tensor_add", xr[:], xr[:], xrt[:])
                p.bar()


BLOCK_FNS = [l0_attn, l0_ffn, l1_attn, l1_ffn]



def l2_attn(p, cx):
    din = cx.din
    nc = cx.nc
    xres_dram = nc.dram_tensor("l2_xres_spill", [128, NT * D], F32)
    kT_in = nc.dram_tensor("l2_kT_in", [B, D, PPC], F32)
    kT_ag = nc.dram_tensor("l2_kT_ag", [NCORES, B, D, PPC], F32)
    v_in = nc.dram_tensor("l2_v_in", [B, PPC, D], F32)
    v_ag = nc.dram_tensor("l2_v_ag", [NCORES, B, PPC, D], F32)
    oT_dram = nc.dram_tensor("l2_oT", [128, NCH * TPC], F32)

    qT_region = cx.x_res
    qTv = qT_region[:].rearrange("p (c t) -> p c t", c=NCH)

    with p.scope() as sc:
        xT = sc.sbuf("l2_xT", [128, NCH * TPC])
        xTv = xT[:].rearrange("p (c t) -> p c t", c=NCH)
        wbuf = sc.sbuf("l2_w", [128, NCH * D])
        wv_ = wbuf[:].rearrange("p (c d) -> p c d", c=NCH)
        xln = sc.sbuf("l2_xln", [128, D])
        ttile = sc.sbuf("l2_tt", [128, D])
        ps_a = sc.psum("l2_ps_a", [128, 512])
        ps_b = sc.psum("l2_ps_b", [128, 512])

        for t in range(NT):
            ln_tile(p, cx, xln[:], cx.xres_t(t), cx.tmp4[:, 4 * t:4 * t + 4])
            for half in range(2):
                for j in range(4):
                    c = half * 4 + j
                    p.t("transpose", cx.tp_ps[:, j * 128:(j + 1) * 128],
                        xln[:, c * 128:(c + 1) * 128], cx.eye[:])
                p.bar()
                for j in range(4):
                    c = half * 4 + j
                    p.s("copy", xTv[:, c, t * 128:(t + 1) * 128], cx.tp_ps[:, j * 128:(j + 1) * 128])
                p.bar()
        p.dma("sync", xres_dram.ap(), cx.x_res[:])
        p.bar()

        for wi, kind in [(0, "q"), (1, "k"), (2, "v")]:
            p.dma("sync", wv_, din["l2_wqkv"][wi].rearrange("(c p) d -> p c d", p=128))
            p.bar()
            for t in range(NT):
                b, tt = t // TPB, t % TPB
                for dh in range(2):
                    ps = ps_a if dh == 0 else ps_b
                    for c in range(NCH):
                        p.mm(ps[:], xTv[:, c, t * 128:(t + 1) * 128],
                             wv_[:, c, dh * 512:(dh + 1) * 512],
                             start=(c == 0), stop=(c == NCH - 1))
                p.bar()
                if kind == "v":
                    p.s("copy", ttile[:, 0:512], ps_a[:])
                    p.s("copy", ttile[:, 512:1024], ps_b[:])
                    p.bar()
                    p.dma("sync", v_in.ap()[b, tt * 128:(tt + 1) * 128, :], ttile[:])
                    p.bar()
                else:
                    if kind == "q":
                        p.v("tensor_scalar", ttile[:, 0:512], ps_a[:], 0.125, None, OP.mult)
                        p.v("tensor_scalar", ttile[:, 512:1024], ps_b[:], 0.125, None, OP.mult)
                    else:
                        p.s("copy", ttile[:, 0:512], ps_a[:])
                        p.s("copy", ttile[:, 512:1024], ps_b[:])
                    p.bar()
                    for half in range(2):
                        for j in range(4):
                            c = half * 4 + j
                            p.t("transpose", cx.tp_ps[:, j * 128:(j + 1) * 128],
                                ttile[:, c * 128:(c + 1) * 128], cx.eye[:])
                        p.bar()
                        for j in range(4):
                            c = half * 4 + j
                            if kind == "q":
                                p.s("copy", qTv[:, c, t * 128:(t + 1) * 128], cx.tp_ps[:, j * 128:(j + 1) * 128])
                            else:
                                p.s("copy", xln[:, c * 128:(c + 1) * 128], cx.tp_ps[:, j * 128:(j + 1) * 128])
                        p.bar()
                    if kind == "k":
                        p.dma("sync", kT_in.ap().rearrange("b (c q) t -> b c q t", q=128)[b, :, :, tt * 128:(tt + 1) * 128].rearrange("c q t -> q c t") if False else kT_in.ap()[b].rearrange("(c q) t -> q c t", q=128)[:, :, tt * 128:(tt + 1) * 128],
                              xln[:].rearrange("q (c t) -> q c t", c=NCH))
                        p.bar()
        p.cc("AllGather", OP.bypass, [kT_in.ap().opt()], [kT_ag.ap().opt()])
        p.bar()
        p.cc("AllGather", OP.bypass, [v_in.ap().opt()], [v_ag.ap().opt()])
        p.bar()

    with p.scope() as sc:
        SKB = 8
        kbuf = [sc.sbuf(f"l2_kb{i}", [128, 512]) for i in range(2)]
        vbuf = [sc.sbuf(f"l2_vb{i}", [128, 4 * 65]) for i in range(2)]
        mcol = sc.sbuf("l2_mcol", [128, 8])
        m11 = sc.sbuf("l2_m11", [1, 8])
        pt = sc.sbuf("l2_pt", [128, 512])
        denr = sc.sbuf("l2_denr", [1, 512])
        ps_s = [sc.psum(f"l2_ps_s{i}", [128, 512]) for i in range(2)]
        ps_o = sc.psum("l2_ps_o", [128, 512])
        ps_m = sc.psum("l2_ps_m2", [128, 128])

        kagv = kT_ag.ap()
        vagv = v_ag.ap()

        for b in range(B):
            for h in range(NH):
                ch, sub = h // 2, (h % 2) * 64
                kslc = lambda i: kbuf[i][sub:sub + 64, :]
                qT_h = qTv[sub:sub + 64, ch, :]
                qb = qT_h[:, b * PPC:(b + 1) * PPC]
                p.v("memset", mcol[:, 0:1], -30000.0)
                p.bar()
                for kb in range(SKB):
                    p.dma("sync", kslc(kb % 2), kagv[kb, b, h * 64:(h + 1) * 64, :])
                    p.bar()
                    for qs in range(4):
                        p.mm(ps_s[qs % 2][:], qb[:, qs * 128:(qs + 1) * 128], kslc(kb % 2),
                             start=True, stop=True)
                        p.bar()
                        p.v("tensor_reduce", mcol[:, 1:2], ps_s[qs % 2][:], AX.X, OP.max)
                        p.bar()
                        p.v("tensor_max", mcol[:, 0:1], mcol[:, 0:1], mcol[:, 1:2])
                        p.bar()
                part_max_scalar(p, cx, m11[0:1, 0:1], mcol[:, 0:1], ps_m)
                bcast_scalar(p, cx, mcol[:, 2:3], m11[0:1, 0:1], ps_m)
                p.v("tensor_scalar", mcol[:, 2:3], mcol[:, 2:3], -1.0, None, OP.mult)
                p.bar()
                for kb in range(SKB):
                    p.dma("sync", kslc(kb % 2), kagv[kb, b, h * 64:(h + 1) * 64, :])
                    vb4 = vbuf[kb % 2][:].rearrange("p (k c) -> p k c", k=4)
                    p.v("memset", vb4[:, :, 64:65], 1.0)
                    p.bar()
                    p.dma("sync", vb4[:, :, 0:64],
                          vagv[kb, b, :, h * 64:(h + 1) * 64].rearrange("(k p) d -> p k d", p=128))
                    p.bar()
                    for ks in range(4):
                        p.mm(ps_s[ks % 2][:], kslc(kb % 2)[:, ks * 128:(ks + 1) * 128], qb[:],
                             start=True, stop=True)
                        p.bar()
                        p.act(pt[:], ps_s[ks % 2][:], AF.Exp, bias=mcol[:, 2:3])
                        p.bar()
                        p.mm(ps_o[0:65, :], vb4[:, ks, :], pt[:],
                             start=(kb == 0 and ks == 0), stop=(kb == SKB - 1 and ks == 3))
                        p.bar()
                p.v("reciprocal", pt[64:65, :], ps_o[64:65, :])
                p.bar()
                p.dma("sync", denr[:], pt[64:65, :])
                p.bar()
                bcast_row(p, cx, ps_s[0][:], denr[0:1, :])
                p.bar()
                p.s("copy", pt[0:64, :], ps_o[0:64, :])
                p.bar()
                p.v("tensor_tensor", pt[0:64, :], pt[0:64, :], ps_s[0][0:64, :], OP.mult)
                p.bar()
                p.dma("sync", oT_dram.ap().rearrange("p (c t) -> p c t", c=NCH)[sub:sub + 64, ch, b * PPC:(b + 1) * PPC],
                      pt[0:64, :])
                p.bar()

    with p.scope() as sc:
        oT = sc.sbuf("l2_oTs", [128, NCH * TPC])
        oTv = oT[:].rearrange("p (c t) -> p c t", c=NCH)
        wbuf = sc.sbuf("l2_wo", [128, NCH * D])
        wv_ = wbuf[:].rearrange("p (c d) -> p c d", c=NCH)
        xrt = sc.sbuf("l2_xrt", [128, D])
        ps_a = sc.psum("l2_ps3_a", [128, 512])
        ps_b = sc.psum("l2_ps3_b", [128, 512])
        p.dma("sync", oT[:], oT_dram.ap())
        p.dma("sync", wv_, din["l2_wqkv"][3].rearrange("(c p) d -> p c d", p=128))
        p.bar()
        for t in range(NT):
            for dh in range(2):
                ps = ps_a if dh == 0 else ps_b
                for c in range(NCH):
                    p.mm(ps[:], oTv[:, c, t * 128:(t + 1) * 128],
                         wv_[:, c, dh * 512:(dh + 1) * 512],
                         start=(c == 0), stop=(c == NCH - 1))
            p.bar()
            p.dma("sync", xrt[:], xres_dram.ap().rearrange("p (t f) -> p t f", t=NT)[:, t, :])
            p.bar()
            xr = cx.xres_t(t)
            p.v("tensor_add", xr[:, 0:512], xrt[:, 0:512], ps_a[:])
            p.bar()
            p.v("tensor_add", xr[:, 512:1024], xrt[:, 512:1024], ps_b[:])
            p.bar()


BLOCK_FNS = [l0_attn, l0_ffn, l1_attn, l1_ffn, l2_attn]



def l2_ffn(p, cx):
    din = cx.din
    nc = cx.nc
    FF = 4096
    TG = 4

    with p.scope() as osc:
        xT = osc.sbuf("l2f_xT", [128, NCH * TPC])
        xTv = xT[:].rearrange("p (c t) -> p c t", c=NCH)
        gates = osc.sbuf("l2f_gates", [128, NT * 8])
        msel = osc.sbuf("l2f_msel", [128, NT])
        ps_a = osc.psum("l2f2_ps_a", [128, 512])
        ps_b = osc.psum("l2f2_ps_b", [128, 512])

        with p.scope() as sc:
            wgr = sc.sbuf("l2f_wgr", [128, NCH * 16])
            wgrv = wgr[:].rearrange("p (c d) -> p c d", c=NCH)
            wg8 = sc.sbuf("l2f_wg8", [8, 16])
            sq8 = sc.sbuf("l2f_sq8", [8, 16])
            wgu = sc.sbuf("l2f_wgu", [16, 8])
            nrm = sc.sbuf("l2f_nrm", [8, 4])
            xln = sc.sbuf("l2f_xln", [128, D])
            xrT = sc.sbuf("l2f_xrT", [16, 128])
            ps_t = sc.psum("l2f_ps_t", [128, 128])
            p.dma("sync", wgrv, din["l2f_wgred"].rearrange("(c p) d -> p c d", p=128))
            p.dma("sync", wg8[:], din["l2f_wg"])
            p.bar()
            p.act(sq8[:], wg8[:], AF.Square)
            p.bar()
            p.v("tensor_reduce", nrm[:, 0:1], sq8[:], AX.X, OP.add)
            p.bar()
            p.act(nrm[:, 1:2], nrm[:, 0:1], AF.Sqrt)
            p.bar()
            p.v("tensor_scalar", nrm[:, 1:2], nrm[:, 1:2], 1e-4, None, OP.max)
            p.bar()
            p.v("reciprocal", nrm[:, 2:3], nrm[:, 1:2])
            p.bar()
            p.v("tensor_scalar", wg8[:], wg8[:], nrm[:, 2:3], None, OP.mult)
            p.bar()
            p.t("transpose", ps_t[0:16, 0:8], wg8[:], cx.eye[0:8, 0:8])
            p.bar()
            p.s("copy", wgu[:], ps_t[0:16, 0:8])
            p.bar()
            for t in range(NT):
                ln_tile(p, cx, xln[:], cx.xres_t(t), cx.tmp4[:, 4 * t:4 * t + 4])
                for half in range(2):
                    for j in range(4):
                        c = half * 4 + j
                        p.t("transpose", cx.tp_ps[:, j * 128:(j + 1) * 128],
                            xln[:, c * 128:(c + 1) * 128], cx.eye[:])
                    p.bar()
                    for j in range(4):
                        c = half * 4 + j
                        p.s("copy", xTv[:, c, t * 128:(t + 1) * 128], cx.tp_ps[:, j * 128:(j + 1) * 128])
                    p.bar()
                for c in range(NCH):
                    p.mm(ps_t[:, 0:16], xTv[:, c, t * 128:(t + 1) * 128], wgrv[:, c, :],
                         start=(c == 0), stop=(c == NCH - 1))
                p.bar()
                p.s("copy", xln[:, 0:16], ps_t[:, 0:16])
                p.bar()
                p.t("transpose", ps_t[0:16, 0:128], xln[:, 0:16], cx.eye[:])
                p.bar()
                p.s("copy", xrT[:], ps_t[0:16, 0:128])
                p.bar()
                p.mm(ps_t[:, 16:24], xrT[:], wgu[:], start=True, stop=True)
                p.bar()
                g8 = gates[:, t * 8:(t + 1) * 8]
                p.v("tensor_reduce", xln[:, 0:1], ps_t[:, 16:24], AX.X, OP.max)
                p.bar()
                p.v("tensor_tensor", g8, ps_t[:, 16:24], xln[:, 0:1].broadcast_to((128, 8)), OP.subtract)
                p.bar()
                p.act(g8, g8, AF.Exp)
                p.bar()
                p.v("tensor_reduce", xln[:, 1:2], g8, AX.X, OP.add)
                p.bar()
                p.v("reciprocal", xln[:, 1:2], xln[:, 1:2])
                p.bar()
                p.v("tensor_scalar", g8, g8, xln[:, 1:2], None, OP.mult)
                p.bar()
                p.v("tensor_reduce", msel[:, t:t + 1], g8, AX.X, OP.max)
                p.bar()

        with p.scope() as sc:
            wb = sc.sbuf("l2f_wb", [128, NCH * 1024])
            wbv = wb[:].rearrange("p (c d) -> p c d", c=NCH)
            htg = sc.sbuf("l2f_htg", [128, TG * 1024])
            acc = sc.sbuf("l2f_acc", [128, TG * 1024])
            hT = sc.sbuf("l2f_hT", [128, 1024])
            b1r = sc.sbuf("l2f_b1r", [1, 1024])
            b2r = sc.sbuf("l2f_b2r", [1, D])
            fac = sc.sbuf("l2f_fac", [128, 1])

            for e in range(8):
                p.dma("sync", b2r[:], din["l2f_b2"][e:e + 1, :])
                p.bar()
                for tg in range(NT // TG):
                    for g in range(4):
                        p.dma("sync", wbv, din["l2f_w1"][e].rearrange("(c p) d -> p c d", p=128)[:, :, g * 1024:(g + 1) * 1024])
                        p.dma("sync", b1r[:], din["l2f_b1"][e:e + 1, g * 1024:(g + 1) * 1024])
                        p.bar()
                        for ti in range(TG):
                            t = tg * TG + ti
                            for dh in range(2):
                                ps = ps_a if dh == 0 else ps_b
                                for c in range(NCH):
                                    p.mm(ps[:], xTv[:, c, t * 128:(t + 1) * 128],
                                         wbv[:, c, dh * 512:(dh + 1) * 512],
                                         start=(c == 0), stop=False)
                                bcast_row(p, cx, ps[:], b1r[0:1, dh * 512:(dh + 1) * 512],
                                          start=False, stop=True)
                            p.bar()
                            p.act(htg[:, ti * 1024:ti * 1024 + 512], ps_a[:], AF.Gelu)
                            p.act(htg[:, ti * 1024 + 512:(ti + 1) * 1024], ps_b[:], AF.Gelu)
                            p.bar()
                        p.dma("sync", wbv, din["l2f_w2"][e].rearrange("(g c p) d -> g p c d", g=4, p=128)[g])
                        p.bar()
                        for ti in range(TG):
                            t = tg * TG + ti
                            for half in range(2):
                                for j in range(4):
                                    c = half * 4 + j
                                    p.t("transpose", cx.tp_ps[:, j * 128:(j + 1) * 128],
                                        htg[:, ti * 1024 + c * 128:ti * 1024 + (c + 1) * 128], cx.eye[:])
                                p.bar()
                                for j in range(4):
                                    c = half * 4 + j
                                    p.s("copy", hT[:, c * 128:(c + 1) * 128], cx.tp_ps[:, j * 128:(j + 1) * 128])
                                p.bar()
                            for dh in range(2):
                                ps = ps_a if dh == 0 else ps_b
                                for c in range(NCH):
                                    p.mm(ps[:], hT[:, c * 128:(c + 1) * 128],
                                         wbv[:, c, dh * 512:(dh + 1) * 512],
                                         start=(c == 0), stop=(c == NCH - 1))
                            p.bar()
                            a_sl = acc[:, ti * 1024:(ti + 1) * 1024]
                            if g == 0:
                                p.s("copy", a_sl[:, 0:512], ps_a[:])
                                p.s("copy", a_sl[:, 512:1024], ps_b[:])
                            else:
                                p.v("tensor_add", a_sl[:, 0:512], a_sl[:, 0:512], ps_a[:])
                                p.bar()
                                p.v("tensor_add", a_sl[:, 512:1024], a_sl[:, 512:1024], ps_b[:])
                            p.bar()
                    for ti in range(TG):
                        t = tg * TG + ti
                        a_sl = acc[:, ti * 1024:(ti + 1) * 1024]
                        bcast_row(p, cx, ps_a[:], b2r[0:1, 0:512])
                        bcast_row(p, cx, ps_b[:], b2r[0:1, 512:1024])
                        p.bar()
                        p.v("tensor_add", a_sl[:, 0:512], a_sl[:, 0:512], ps_a[:])
                        p.bar()
                        p.v("tensor_add", a_sl[:, 512:1024], a_sl[:, 512:1024], ps_b[:])
                        p.bar()
                        p.v("tensor_tensor", fac[:], gates[:, t * 8 + e:t * 8 + e + 1],
                            msel[:, t:t + 1], OP.is_ge)
                        p.bar()
                        p.v("tensor_mul", fac[:], fac[:], msel[:, t:t + 1])
                        p.bar()
                        p.v("tensor_scalar", a_sl, a_sl, fac[:], None, OP.mult)
                        p.bar()
                        xr = cx.xres_t(t)
                        p.v("tensor_add", xr[:], xr[:], a_sl)
                        p.bar()


BLOCK_FNS = [l0_attn, l0_ffn, l1_attn, l1_ffn, l2_attn, l2_ffn]



def l3_ffn(p, cx):
    din = cx.din
    nc = cx.nc
    FF = 4096
    TG = 4
    KSEL = 4096

    sc_in = nc.dram_tensor("l3f_sc_in", [TPC, 8], F32)
    sc_ag = nc.dram_tensor("l3f_sc_ag", [NCORES * TPC, 8], F32)

    with p.scope() as osc:
        xT = osc.sbuf("l3f_xT", [128, NCH * TPC])
        xTv = xT[:].rearrange("p (c t) -> p c t", c=NCH)
        svals = osc.sbuf("l3f_svals", [128, NT * 8])
        thrb = osc.sbuf("l3f_thrb", [128, 8])
        ps_a = osc.psum("l3f3_ps_a", [128, 512])
        ps_b = osc.psum("l3f3_ps_b", [128, 512])

        with p.scope() as sc:
            wgb = sc.sbuf("l3f_wgb", [128, NCH * 8])
            wgv = wgb[:].rearrange("p (c d) -> p c d", c=NCH)
            xln = sc.sbuf("l3f_xln", [128, D])
            ps_t = sc.psum("l3f_ps_t", [128, 128])
            p.dma("sync", wgv, din["l3f_wg"].rearrange("(c p) d -> p c d", p=128))
            p.bar()
            for t in range(NT):
                ln_tile(p, cx, xln[:], cx.xres_t(t), cx.tmp4[:, 4 * t:4 * t + 4])
                for half in range(2):
                    for j in range(4):
                        c = half * 4 + j
                        p.t("transpose", cx.tp_ps[:, j * 128:(j + 1) * 128],
                            xln[:, c * 128:(c + 1) * 128], cx.eye[:])
                    p.bar()
                    for j in range(4):
                        c = half * 4 + j
                        p.s("copy", xTv[:, c, t * 128:(t + 1) * 128], cx.tp_ps[:, j * 128:(j + 1) * 128])
                    p.bar()
                for c in range(NCH):
                    p.mm(ps_t[:, 0:8], xTv[:, c, t * 128:(t + 1) * 128], wgv[:, c, :],
                         start=(c == 0), stop=(c == NCH - 1))
                p.bar()
                p.act(svals[:, t * 8:(t + 1) * 8], ps_t[:, 0:8], AF.Sigmoid)
                p.bar()
                p.dma("sync", sc_in.ap()[t * 128:(t + 1) * 128, :], svals[:, t * 8:(t + 1) * 8])
                p.bar()
            p.cc("AllGather", OP.bypass, [sc_in.ap().opt()], [sc_ag.ap().opt()])
            p.bar()

        with p.scope() as sc:
            sall = sc.sbuf("l3f_sall", [128, 128 * 8])
            lo = sc.sbuf("l3f_lo", [1, 8])
            hi = sc.sbuf("l3f_hi", [1, 8])
            mid = sc.sbuf("l3f_mid", [1, 8])
            midb = sc.sbuf("l3f_midb", [128, 8])
            cnt = sc.sbuf("l3f_cnt", [128, 8])
            cnt1 = sc.sbuf("l3f_cnt1", [1, 8])
            selm = sc.sbuf("l3f_selm", [1, 8])
            mbuf = sc.sbuf("l3f_mbuf", [128, 8 * 128])
            onesc = sc.sbuf("l3f_onesc", [128, 1])
            ps_t = sc.psum("l3f_ps_t2", [128, 128])
            p.dma("sync", sall[:].rearrange("p (e a) -> p e a", e=8),
                  sc_ag.ap().rearrange("(a p) e -> p e a", p=128), allow_slow_non_contiguous=True)
            p.v("memset", lo[:], 0.0)
            p.v("memset", hi[:], 1.0)
            p.v("memset", onesc[:], 1.0)
            p.bar()
            s3 = sall[:].rearrange("p (e a) -> p e a", e=8)
            m3 = mbuf[:].rearrange("p (e a) -> p e a", e=8)
            for it in range(48):
                p.v("tensor_add", mid[:], lo[:], hi[:])
                p.bar()
                p.v("tensor_scalar", mid[:], mid[:], 0.5, None, OP.mult)
                p.bar()
                p.mm(ps_t[:, 0:8], cx.ones_row[:, 0:128], mid[:], start=True, stop=True)
                p.bar()
                p.s("copy", midb[:], ps_t[:, 0:8])
                p.bar()
                p.v("tensor_tensor", m3, s3, midb[:].unsqueeze(2).broadcast_to((128, 8, 128)), OP.is_ge)
                p.bar()
                p.v("tensor_reduce", cnt[:], m3, AX.X, OP.add)
                p.bar()
                p.mm(ps_t[0:1, 8:16], onesc[:], cnt[:], start=True, stop=True)
                p.bar()
                p.v("tensor_scalar", cnt1[:], ps_t[0:1, 8:16], float(KSEL), None, OP.is_ge)
                p.bar()
                p.v("tensor_sub", selm[:], mid[:], lo[:])
                p.bar()
                p.v("tensor_mul", selm[:], selm[:], cnt1[:])
                p.bar()
                p.v("tensor_add", lo[:], lo[:], selm[:])
                p.bar()
                p.v("tensor_sub", selm[:], hi[:], mid[:])
                p.bar()
                p.v("tensor_mul", selm[:], selm[:], cnt1[:])
                p.bar()
                p.v("tensor_add", hi[:], selm[:], mid[:])
                p.bar()
            p.mm(ps_t[:, 0:8], cx.ones_row[:, 0:128], lo[:], start=True, stop=True)
            p.bar()
            p.s("copy", thrb[:], ps_t[:, 0:8])
            p.bar()

        with p.scope() as sc:
            wb = sc.sbuf("l3f_wb", [128, NCH * 1024])
            wbv = wb[:].rearrange("p (c d) -> p c d", c=NCH)
            htg = sc.sbuf("l3f_htg", [128, TG * 1024])
            acc = sc.sbuf("l3f_acc", [128, TG * 1024])
            hT = sc.sbuf("l3f_hT", [128, 1024])
            b1r = sc.sbuf("l3f_b1r", [1, 1024])
            b2r = sc.sbuf("l3f_b2r", [1, D])
            fac = sc.sbuf("l3f_fac", [128, 1])

            for e in range(8):
                p.dma("sync", b2r[:], din["l3f_b2"][e:e + 1, :])
                p.bar()
                for tg in range(NT // TG):
                    for g in range(4):
                        p.dma("sync", wbv, din["l3f_w1"][e].rearrange("(c p) d -> p c d", p=128)[:, :, g * 1024:(g + 1) * 1024])
                        p.dma("sync", b1r[:], din["l3f_b1"][e:e + 1, g * 1024:(g + 1) * 1024])
                        p.bar()
                        for ti in range(TG):
                            t = tg * TG + ti
                            for dh in range(2):
                                ps = ps_a if dh == 0 else ps_b
                                for c in range(NCH):
                                    p.mm(ps[:], xTv[:, c, t * 128:(t + 1) * 128],
                                         wbv[:, c, dh * 512:(dh + 1) * 512],
                                         start=(c == 0), stop=False)
                                bcast_row(p, cx, ps[:], b1r[0:1, dh * 512:(dh + 1) * 512],
                                          start=False, stop=True)
                            p.bar()
                            p.act(htg[:, ti * 1024:ti * 1024 + 512], ps_a[:], AF.Gelu)
                            p.act(htg[:, ti * 1024 + 512:(ti + 1) * 1024], ps_b[:], AF.Gelu)
                            p.bar()
                        p.dma("sync", wbv, din["l3f_w2"][e].rearrange("(g c p) d -> g p c d", g=4, p=128)[g])
                        p.bar()
                        for ti in range(TG):
                            t = tg * TG + ti
                            for half in range(2):
                                for j in range(4):
                                    c = half * 4 + j
                                    p.t("transpose", cx.tp_ps[:, j * 128:(j + 1) * 128],
                                        htg[:, ti * 1024 + c * 128:ti * 1024 + (c + 1) * 128], cx.eye[:])
                                p.bar()
                                for j in range(4):
                                    c = half * 4 + j
                                    p.s("copy", hT[:, c * 128:(c + 1) * 128], cx.tp_ps[:, j * 128:(j + 1) * 128])
                                p.bar()
                            for dh in range(2):
                                ps = ps_a if dh == 0 else ps_b
                                for c in range(NCH):
                                    p.mm(ps[:], hT[:, c * 128:(c + 1) * 128],
                                         wbv[:, c, dh * 512:(dh + 1) * 512],
                                         start=(c == 0), stop=(c == NCH - 1))
                            p.bar()
                            a_sl = acc[:, ti * 1024:(ti + 1) * 1024]
                            if g == 0:
                                p.s("copy", a_sl[:, 0:512], ps_a[:])
                                p.s("copy", a_sl[:, 512:1024], ps_b[:])
                            else:
                                p.v("tensor_add", a_sl[:, 0:512], a_sl[:, 0:512], ps_a[:])
                                p.bar()
                                p.v("tensor_add", a_sl[:, 512:1024], a_sl[:, 512:1024], ps_b[:])
                            p.bar()
                    for ti in range(TG):
                        t = tg * TG + ti
                        a_sl = acc[:, ti * 1024:(ti + 1) * 1024]
                        bcast_row(p, cx, ps_a[:], b2r[0:1, 0:512])
                        bcast_row(p, cx, ps_b[:], b2r[0:1, 512:1024])
                        p.bar()
                        p.v("tensor_add", a_sl[:, 0:512], a_sl[:, 0:512], ps_a[:])
                        p.bar()
                        p.v("tensor_add", a_sl[:, 512:1024], a_sl[:, 512:1024], ps_b[:])
                        p.bar()
                        sv = svals[:, t * 8 + e:t * 8 + e + 1]
                        p.v("tensor_tensor", fac[:], sv, thrb[:, e:e + 1], OP.is_ge)
                        p.bar()
                        p.v("tensor_mul", fac[:], fac[:], sv)
                        p.bar()
                        p.v("tensor_scalar", a_sl, a_sl, fac[:], None, OP.mult)
                        p.bar()
                        xr = cx.xres_t(t)
                        p.v("tensor_add", xr[:], xr[:], a_sl)
                        p.bar()


BLOCK_FNS = [l0_attn, l0_ffn, l1_attn, l1_ffn, l2_attn, l2_ffn, l3_ffn]



def l3_attn(p, cx):
    din = cx.din
    nc = cx.nc
    NTEN = 6
    R_, WL_, KK_, V_, K_, B_ = range(6)
    L = 16
    SEG = 1024
    NSEG = S // SEG

    xres_dram = nc.dram_tensor("l3_xres_spill", [128, NT * D], F32)
    wkv_in8a = nc.dram_tensor("l3_wkv_in8a", [8, 3, 512, 512], F32)
    wkv_in8b = nc.dram_tensor("l3_wkv_in8b", [8, 3, 512, 512], F32)
    wkv_ag8a = nc.dram_tensor("l3_wkv_ag8a", [NCORES, 8, 3, 512, 512], F32)
    wkv_ag8b = nc.dram_tensor("l3_wkv_ag8b", [NCORES, 8, 3, 512, 512], F32)
    gT_dram = nc.dram_tensor("l3_gT", [128, NCH * TPC], F32)
    yb_in = nc.dram_tensor("l3_yb_in", [8, 512, 512], F32)
    yb_ag = nc.dram_tensor("l3_yb_ag", [NCORES, 8, 512, 512], F32)
    bnd_in = nc.dram_tensor("l3_bnd_in", [B, D], F32)
    bnd_out = nc.dram_tensor("l3_bnd_out", [NCORES * B, D], F32)

    with p.scope() as sc:
        wful = sc.sbuf("l3_wful", [128, NCH * D])
        wfv = wful[:].rearrange("p (c d) -> p c d", c=NCH)
        wxa = sc.sbuf("l3_wxa", [128, NCH * 192])
        wxav = wxa[:].rearrange("p (c d) -> p c d", c=NCH)
        loA = sc.sbuf("l3_loA", [128, NCH * 64])
        loAv = loA[:].rearrange("p (c d) -> p c d", c=NCH)
        loB = sc.sbuf("l3_loB", [64, D])
        gA = sc.sbuf("l3_gA", [128, NCH * 128])
        gAv = gA[:].rearrange("p (c d) -> p c d", c=NCH)
        gB = sc.sbuf("l3_gB", [128, D])
        xmiu_b = sc.sbuf("l3_xmiub", [128, D])
        misc_b = sc.sbuf("l3_miscb", [128, 2 * D])
        lamb = sc.sbuf("l3_lamb", [1, D])
        xln = sc.sbuf("l3_xln", [128, D])
        xprev = sc.sbuf("l3_xprev", [128, D])
        dxb = sc.sbuf("l3_dx", [128, D])
        xd = sc.sbuf("l3_xd", [128, D])
        xdT = sc.sbuf("l3_xdT", [128, NCH * 128])
        xdTv = lambda c: xdT[:, c * 128:(c + 1) * 128]
        mix = sc.sbuf("l3_mix", [128, 192])
        mixT = sc.sbuf("l3_mixT", [64, 3 * 128])
        shalf = sc.sbuf("l3_shalf", [128, D])
        ktile = sc.sbuf("l3_ktile", [128, D])
        otile = sc.sbuf("l3_otile", [128, D])
        sgb = sc.sbuf("l3_sgb", [128, D])
        t64 = sc.sbuf("l3_t64", [128, 64])
        t64T = sc.sbuf("l3_t64T", [64, 128])
        red = sc.sbuf("l3_red", [128, 4])
        bnd = sc.sbuf("l3_bnd", [B, D])
        selrow = sc.sbuf("l3_sel", [8, 1])
        lntmp = sc.sbuf("l3_lntmp", [B, 4])
        ps_a = sc.psum("l3_ps_a", [128, 512])
        ps_b = sc.psum("l3_ps_b", [128, 512])
        ps_t = sc.psum("l3_ps_t", [128, 192])

        p.dma("sync", selrow[:], din["sel_prev"])
        p.dma("sync", wxav, din["l3_x_A"].rearrange("(c p) d -> p c d", p=128))
        p.dma("sync", gAv, din["l3_g_A"].rearrange("(c p) d -> p c d", p=128))
        p.dma("sync", gB[:], din["l3_g_B"])
        p.bar()
        p.dma("sync", xln[0:1, :], din["l3_misc"][2:3, :])
        p.bar()
        bcast_row(p, cx, ps_a[:], xln[0:1, 0:512])
        bcast_row(p, cx, ps_b[:], xln[0:1, 512:1024])
        p.bar()
        p.s("copy", xmiu_b[:, 0:512], ps_a[:])
        p.s("copy", xmiu_b[:, 512:1024], ps_b[:])
        p.bar()
        for i, nm in enumerate([0, 1]):
            p.dma("sync", xln[0:1, :], din["l3_misc"][i:i + 1, :])
            p.bar()
            bcast_row(p, cx, ps_a[:], xln[0:1, 0:512])
            bcast_row(p, cx, ps_b[:], xln[0:1, 512:1024])
            p.bar()
            p.s("copy", misc_b[:, i * D:i * D + 512], ps_a[:])
            p.s("copy", misc_b[:, i * D + 512:(i + 1) * D], ps_b[:])
            p.bar()

        for b in range(B):
            p.dma("sync", ktile[b:b + 1, :], cx.xres_t(b * TPB + TPB - 1)[127:128, :])
        p.bar()
        ln_tile(p, cx, bnd[:], ktile[0:B, :], lntmp[:])
        p.dma("sync", bnd_in.ap(), bnd[:])
        p.bar()
        p.cc("AllGather", OP.bypass, [bnd_in.ap().opt()], [bnd_out.ap().opt()])
        p.bar()
        for b in range(B):
            p.dma("sync", otile[0:8, :], bnd_out.ap().rearrange("(n b) d -> n b d", b=B)[:, b, :])
            p.bar()
            for h2 in range(2):
                p.mm(ps_a[0:1, :], selrow[:], otile[0:8, h2 * 512:(h2 + 1) * 512],
                     start=True, stop=True)
                p.bar()
                p.s("copy", otile[0:1, h2 * 512:(h2 + 1) * 512], ps_a[0:1, :])
                p.bar()
            p.dma("sync", bnd[b:b + 1, :], otile[0:1, :])
            p.bar()

        prev_xln = sc.sbuf("l3_prevxln", [128, D])
        import os as _os1
        _l3t = int(_os1.environ.get("L3T", "16"))
        for b in range(B):
            for tt in range(TPB):
                t = b * TPB + tt
                tl_ = tt
                if t >= _l3t:
                    continue
                _gset = set(int(c) for c in _os1.environ.get("L3G", "012345"))
                if tt > 0:
                    p.s("copy", prev_xln[:], xln[:])
                    p.bar()
                ln_tile(p, cx, xln[:], cx.xres_t(t), cx.tmp4[:, 4 * t:4 * t + 4])
                p.dma("sync", xprev[1:128, :], xln[0:127, :])
                if tt == 0:
                    p.dma("sync", xprev[0:1, :], bnd[b:b + 1, :])
                else:
                    p.dma("sync", xprev[0:1, :], prev_xln[127:128, :])
                p.bar()
                p.v("tensor_sub", dxb[:], xprev[:], xln[:])
                p.bar()
                p.v("tensor_mul", xd[:], dxb[:], xmiu_b[:])
                p.bar()
                p.v("tensor_add", xd[:], xd[:], xln[:])
                p.bar()
                for half in range(2):
                    for j in range(4):
                        c = half * 4 + j
                        p.t("transpose", cx.tp_ps[:, j * 128:(j + 1) * 128],
                            xd[:, c * 128:(c + 1) * 128], cx.eye[:])
                    p.bar()
                    for j in range(4):
                        c = half * 4 + j
                        p.s("copy", xdTv(c), cx.tp_ps[:, j * 128:(j + 1) * 128])
                    p.bar()
                for c in range(NCH):
                    p.mm(ps_t[:], xdTv(c), wxav[:, c, :], start=(c == 0), stop=(c == NCH - 1))
                p.bar()
                p.act(mix[:], ps_t[:], AF.Tanh)
                p.bar()
                for mb in range(3):
                    p.t("transpose", cx.tp_ps[0:64, 0:128], mix[:, mb * 64:(mb + 1) * 64], cx.eye[:])
                    p.bar()
                    p.s("copy", mixT[:, mb * 128:(mb + 1) * 128], cx.tp_ps[0:64, 0:128])
                    p.bar()

                def build_xd(g):
                    p.dma("sync", lamb[:], din["l3_lambdas"].rearrange("g a b d -> (g a b) d")[g:g + 1, :])
                    p.bar()
                    mb_ = 32 * (g % 2)
                    p.dma("sync", t64T[0:32, 0:128], mixT[mb_:mb_ + 32, (g // 2) * 128:(g // 2 + 1) * 128])
                    p.bar()
                    for dh in range(2):
                        ps = ps_a if dh == 0 else ps_b
                        p.dma("sync", loB[0:32, 0:512], din["l3_x_B"][g, :, dh * 512:(dh + 1) * 512])
                        p.bar()
                        p.mm(ps[:], t64T[0:32, 0:128], loB[0:32, 0:512], start=True, stop=False)
                        bcast_row(p, cx, ps[:], lamb[0:1, dh * 512:(dh + 1) * 512], start=False, stop=True)
                        p.bar()
                        p.v("tensor_tensor", xd[:, dh * 512:(dh + 1) * 512], dxb[:, dh * 512:(dh + 1) * 512], ps[:], OP.mult)
                        p.bar()
                    p.v("tensor_add", xd[:], xd[:], xln[:])
                    p.bar()
                    for half in range(2):
                        for j in range(4):
                            c = half * 4 + j
                            p.t("transpose", cx.tp_ps[:, j * 128:(j + 1) * 128],
                                xd[:, c * 128:(c + 1) * 128], cx.eye[:])
                        p.bar()
                        for j in range(4):
                            c = half * 4 + j
                            p.s("copy", xdTv(c), cx.tp_ps[:, j * 128:(j + 1) * 128])
                        p.bar()

                def full_proj(widx, outbuf):
                    for dh in range(2):
                        ps = ps_a if dh == 0 else ps_b
                        for c in range(NCH):
                            p.mm(ps[:], xdTv(c), wfv[:, c, dh * 512:(dh + 1) * 512],
                                 start=(c == 0), stop=(c == NCH - 1))
                    p.bar()
                    p.s("copy", outbuf[:, 0:512], ps_a[:])
                    p.s("copy", outbuf[:, 512:1024], ps_b[:])
                    p.bar()

                def store_fm(tidx, srcbuf):
                    for half in range(2):
                        for j in range(4):
                            c = half * 4 + j
                            p.t("transpose", cx.tp_ps[:, j * 128:(j + 1) * 128],
                                srcbuf[:, c * 128:(c + 1) * 128], cx.eye[:])
                        p.bar()
                        for j in range(4):
                            c = half * 4 + j
                            p.s("copy", otile[:, c * 128:(c + 1) * 128], cx.tp_ps[:, j * 128:(j + 1) * 128])
                        p.bar()
                    w8 = wkv_in8a if tidx < 3 else wkv_in8b
                    for hh in range(2):
                        slot = 2 * b + hh
                        p.dma("sync",
                              w8.ap()[slot, tidx % 3].rearrange("(c q) tk -> q c tk", q=128)[:, :, tl_ * 128:(tl_ + 1) * 128],
                              otile[:, hh * 512:(hh + 1) * 512].rearrange("q (c tk) -> q c tk", c=4))
                    p.bar()

                def lowrank_stage1(a_name):
                    p.dma("sync", loAv, din[a_name].rearrange("(c p) d -> p c d", p=128))
                    p.bar()
                    for c in range(NCH):
                        p.mm(ps_t[:, 0:64], xdTv(c), loAv[:, c, :], start=(c == 0), stop=(c == NCH - 1))
                    p.bar()
                    p.act(t64[:], ps_t[:, 0:64], AF.Tanh)
                    p.bar()
                    p.t("transpose", cx.tp_ps[0:64, 0:128], t64[:], cx.eye[:])
                    p.bar()
                    p.s("copy", t64T[:], cx.tp_ps[0:64, 0:128])
                    p.bar()

                if 1 not in _gset:
                    continue
                build_xd(1)
                lowrank_stage1("l3_w_A")
                p.dma("sync", loB[:], din["l3_w_B"])
                p.dma("sync", lamb[:], din["l3_misc"][0:1, :])
                p.bar()
                for dh in range(2):
                    ps = ps_a if dh == 0 else ps_b
                    p.mm(ps[:], t64T[:], loB[:, dh * 512:(dh + 1) * 512], start=True, stop=False)
                    bcast_row(p, cx, ps[:], lamb[0:1, dh * 512:(dh + 1) * 512], start=False, stop=True)
                p.bar()
                p.act(sgb[:, 0:512], ps_a[:], AF.Sigmoid)
                p.act(sgb[:, 512:1024], ps_b[:], AF.Sigmoid)
                p.bar()
                p.act(xd[:], sgb[:], AF.Ln)
                p.bar()
                p.v("tensor_scalar", xd[:], xd[:], -0.5, None, OP.add)
                p.bar()
                store_fm(WL_, xd)
                p.act(shalf[:], sgb[:], AF.Sqrt)
                p.bar()
                p.v("tensor_scalar", shalf[:], shalf[:], float(np.exp(-0.25)), None, OP.mult)
                p.bar()

                if 2 not in _gset:
                    continue
                build_xd(2)
                lowrank_stage1("l3_k_A")
                p.dma("sync", loB[:], din["l3_k_B"])
                p.dma("sync", wfv, din["l3_wrkvo"][1].rearrange("(c p) d -> p c d", p=128))
                p.bar()
                for dh in range(2):
                    ps = ps_a if dh == 0 else ps_b
                    for c in range(NCH):
                        p.mm(ps[:], xdTv(c), wfv[:, c, dh * 512:(dh + 1) * 512],
                             start=(c == 0), stop=False)
                    p.mm(ps[:], t64T[:], loB[:, dh * 512:(dh + 1) * 512], start=False, stop=True)
                p.bar()
                p.s("copy", ktile[:, 0:512], ps_a[:])
                p.s("copy", ktile[:, 512:1024], ps_b[:])
                p.bar()
                p.act(xd[:], ktile[:], AF.Square)
                p.bar()
                p.v("tensor_reduce", red[:, 0:1], xd[:], AX.X, OP.add)
                p.bar()
                p.act(red[:, 1:2], red[:, 0:1], AF.Sqrt)
                p.bar()
                p.v("tensor_scalar", red[:, 1:2], red[:, 1:2], 1e-12, None, OP.max)
                p.bar()
                p.v("reciprocal", red[:, 1:2], red[:, 1:2])
                p.bar()
                p.v("tensor_scalar", ktile[:], ktile[:], red[:, 1:2], None, OP.mult)
                p.bar()
                store_fm(K_, ktile)
                p.v("tensor_mul", xd[:], ktile[:], shalf[:])
                p.bar()
                store_fm(KK_, xd)

                if 4 not in _gset:
                    continue
                build_xd(4)
                lowrank_stage1("l3_a_A")
                p.dma("sync", loB[:], din["l3_a_B"])
                p.dma("sync", lamb[:], din["l3_misc"][1:2, :])
                p.bar()
                for dh in range(2):
                    ps = ps_a if dh == 0 else ps_b
                    p.mm(ps[:], t64T[:], loB[:, dh * 512:(dh + 1) * 512], start=True, stop=False)
                    bcast_row(p, cx, ps[:], lamb[0:1, dh * 512:(dh + 1) * 512], start=False, stop=True)
                p.bar()
                p.act(xd[:, 0:512], ps_a[:], AF.Sigmoid)
                p.act(xd[:, 512:1024], ps_b[:], AF.Sigmoid)
                p.bar()
                p.v("tensor_scalar", xd[:], xd[:], 2.0, None, OP.mult)
                p.bar()
                p.v("tensor_mul", xd[:], xd[:], ktile[:])
                p.bar()
                store_fm(B_, xd)

                for g, widx, tidx in [(3, 2, V_), (0, 0, R_)]:
                    if g not in _gset:
                        continue
                    build_xd(g)
                    p.dma("sync", wfv, din["l3_wrkvo"][widx].rearrange("(c p) d -> p c d", p=128))
                    p.bar()
                    for dh in range(2):
                        ps = ps_a if dh == 0 else ps_b
                        for c in range(NCH):
                            p.mm(ps[:], xdTv(c), wfv[:, c, dh * 512:(dh + 1) * 512],
                                 start=(c == 0), stop=(c == NCH - 1))
                    p.bar()
                    p.s("copy", xd[:, 0:512], ps_a[:])
                    p.s("copy", xd[:, 512:1024], ps_b[:])
                    p.bar()
                    store_fm(tidx, xd)

                if 5 not in _gset:
                    continue
                build_xd(5)
                for c in range(NCH):
                    p.mm(ps_t[:, 0:128], xdTv(c), gAv[:, c, :], start=(c == 0), stop=(c == NCH - 1))
                p.bar()
                p.act(ktile[:, 0:128], ps_t[:, 0:128], AF.Tanh)
                p.bar()
                p.t("transpose", cx.tp_ps[:, 0:128], ktile[:, 0:128], cx.eye[:])
                p.bar()
                p.s("copy", xdT[:, 0:128], cx.tp_ps[:, 0:128])
                p.bar()
                for dh in range(2):
                    ps = ps_a if dh == 0 else ps_b
                    p.mm(ps[:], xdT[:, 0:128], gB[:, dh * 512:(dh + 1) * 512], start=True, stop=True)
                p.bar()
                p.s("copy", xd[:, 0:512], ps_a[:])
                p.s("copy", xd[:, 512:1024], ps_b[:])
                p.bar()
                for half in range(2):
                    for j in range(4):
                        c = half * 4 + j
                        p.t("transpose", cx.tp_ps[:, j * 128:(j + 1) * 128],
                            xd[:, c * 128:(c + 1) * 128], cx.eye[:])
                    p.bar()
                    for j in range(4):
                        c = half * 4 + j
                        p.s("copy", otile[:, c * 128:(c + 1) * 128], cx.tp_ps[:, j * 128:(j + 1) * 128])
                    p.bar()
                p.dma("sync", gT_dram.ap().rearrange("q (c tk) -> q c tk", c=NCH)[:, :, t * 128:(t + 1) * 128],
                      otile[:].rearrange("q (c tk) -> q c tk", c=NCH))
                p.bar()

        import os as _os0
        p.dma("sync", xres_dram.ap(), cx.x_res[:])
        p.bar()
        if int(_os0.environ.get("L3P", "3")) >= 1:
            p.cc("AllGather", OP.bypass, [wkv_in8a.ap().opt()], [wkv_ag8a.ap().opt()])
            p.bar()
            p.cc("AllGather", OP.bypass, [wkv_in8b.ap().opt()], [wkv_ag8b.ap().opt()])
            p.bar()

    import os as _os
    _l3p = int(_os.environ.get("L3P", "3"))
    if _l3p < 2:
        p.dma("sync", cx.x_res[:], xres_dram.ap())
        p.bar()
        return
    pid = nc.partition_id()
    bb_sv = pid // 2
    hh_sv = pid % 2
    gb_my = nc.dram_tensor("l3_gb_my", [3, 512], F32)
    wkv_my = nc.dram_tensor("l3_wkv_myb", [NCORES, NTEN, 512, 512], F32)
    yb_my = nc.dram_tensor("l3_yb_myb", [NCORES, 512, 512], F32)
    with p.scope() as sc:
        fm = sc.sbuf("l3w_fm", [128, NTEN * SEG])
        fmv = fm[:].rearrange("p (n t) -> p n t", n=NTEN)
        Wc = sc.sbuf("l3w_W", [128, SEG])
        Wsh1 = sc.sbuf("l3w_Wsh1", [128, SEG])
        e1 = sc.sbuf("l3w_e1", [128, SEG])
        e0 = sc.sbuf("l3w_e0", [128, SEG])
        eck = sc.sbuf("l3w_eck", [128, SEG])
        av = sc.sbuf("l3w_av", [128, SEG])
        rv = sc.sbuf("l3w_rv", [128, SEG])
        ktl = sc.sbuf("l3w_ktl", [128, SEG])
        btl = sc.sbuf("l3w_btl", [128, SEG])
        base = sc.sbuf("l3w_base", [128, SEG])
        uloc = sc.sbuf("l3w_uloc", [128, SEG])
        ahat = sc.sbuf("l3w_ahat", [128, SEG])
        ubuf = sc.sbuf("l3w_u", [128, SEG])
        ybuf = sc.sbuf("l3w_y", [128, SEG])
        esub = sc.sbuf("l3w_esub", [128, SEG])
        prod1 = sc.sbuf("l3w_prod1", [128, SEG])
        prod2 = sc.sbuf("l3w_prod2", [128, SEG])
        mask = sc.sbuf("l3w_mask", [128, SEG])
        babc = cx.x_res[:, 0:15 * SEG]
        scb = sc.sbuf("l3w_scb", [128, SEG])
        bd64 = sc.sbuf("l3w_bd64", [128, 128])
        Sb = sc.sbuf("l3w_S", [128, 64])
        uT = sc.sbuf("l3w_uT", [16, 128])
        vT = sc.sbuf("l3w_vT", [16, 128])
        kT_ = sc.sbuf("l3w_kT", [16, 128])
        bT_ = sc.sbuf("l3w_bT", [16, 128])
        gcol = sc.sbuf("l3w_gcol", [128, 4])
        rowsA = sc.sbuf("l3w_rowsA", [2, 6 * SEG])
        bd2T = sc.sbuf("l3w_bd2T", [2, 128])
        ps_w = sc.psum("l3w_ps_w", [128, 512])
        ps_s = sc.psum("l3w_ps_s", [128, 64])
        ps_u = sc.psum("l3w_ps_u", [128, 16])

        p.dma("sync", bd64[:], din["bd64"])
        p.dma("sync", bd2T[:], din["bd2T"])
        p.bar()
        for n in range(NCORES):
            p.dma("sync", wkv_my.ap()[n, 0:3], wkv_ag8a.ap()[n][bass.ts(pid, 1)].squeeze(0))
            p.dma("sync", wkv_my.ap()[n, 3:6], wkv_ag8b.ap()[n][bass.ts(pid, 1)].squeeze(0))
        p.dma("sync", gb_my.ap()[0:1, :], din["l3_gn_gamma"].unsqueeze(0)[:, bass.ts(hh_sv, 512)])
        p.dma("sync", gb_my.ap()[1:2, :], din["l3_gn_beta"].unsqueeze(0)[:, bass.ts(hh_sv, 512)])
        p.dma("sync", gb_my.ap()[2:3, :],
              din["l3_x_time"].rearrange("h d -> (h d)").unsqueeze(0)[:, bass.ts(hh_sv, 512)])
        p.bar()


        def banded_sc(dst_ps, prod, other, negate):
            p.v("tensor_mul", prod2[:], prod, other)
            p.bar()
            for dh in range(2):
                p.mm(dst_ps[:], bd64[:], prod2[:, dh * 512:(dh + 1) * 512], start=True, stop=True)
                p.bar()
                if negate:
                    p.v("tensor_scalar", scb[:, dh * 512:(dh + 1) * 512], dst_ps[:], -1.0, None, OP.mult)
                else:
                    p.s("copy", scb[:, dh * 512:(dh + 1) * 512], dst_ps[:])
                p.bar()

        for pg in range(4):
            p.v("memset", Sb[:], 0.0)
            p.bar()
            for seg in range(NSEG):
                for ten in range(NTEN):
                    for j in range(2):
                        rb = (2 * pg + j) * 64
                        for half in range(2):
                            n = 2 * seg + half
                            p.dma("sync", fmv[j * 64:(j + 1) * 64, ten, half * 512:(half + 1) * 512],
                                  wkv_my.ap()[n, ten, rb:rb + 64, :])
                p.bar()
                for ck in range(SEG // L):
                    p.v("tensor_tensor_scan", Wc[:, ck * L:(ck + 1) * L],
                        fmv[:, WL_, ck * L:(ck + 1) * L], fmv[:, WL_, ck * L:(ck + 1) * L],
                        0.0, OP.add, OP.bypass)
                p.bar()
                p.dma("sync", mask[:], din["l3_dmask"][1])
                p.bar()
                p.act(e1[:], Wc[:], AF.Exp)
                p.bar()
                p.v("memset", Wsh1[:, 0:1], 0.0)
                p.bar()
                p.v("tensor_copy", Wsh1[:, 1:SEG], Wc[:, 0:SEG - 1])
                p.bar()
                p.v("tensor_mul", Wsh1[:], Wsh1[:], mask[:])
                p.bar()
                p.act(e0[:], Wsh1[:], AF.Exp)
                p.bar()
                p.v("tensor_copy", esub[:].rearrange("p (c k) -> p c k", k=L),
                    Wc[:, L - 1::L].unsqueeze(2).broadcast_to((128, SEG // L, L)))
                p.bar()
                p.v("tensor_sub", esub[:], esub[:], Wc[:])
                p.bar()
                p.act(eck[:], esub[:], AF.Exp)
                p.bar()
                p.v("tensor_mul", av[:], fmv[:, K_, :], e0[:])
                p.bar()
                p.v("tensor_scalar", av[:], av[:], -1.0, None, OP.mult)
                p.bar()
                p.v("tensor_mul", rv[:], fmv[:, R_, :], e1[:])
                p.bar()
                p.v("tensor_mul", ktl[:], fmv[:, KK_, :], eck[:])
                p.bar()
                p.v("tensor_mul", btl[:], fmv[:, B_, :], eck[:])
                p.bar()
                p.v("memset", base[:], 0.0)
                p.bar()
                for dlt in range(1, 16):
                    p.dma("sync", mask[:], din["l3_dmask"][dlt])
                    p.bar()
                    p.v("memset", esub[:, 0:dlt], 0.0)
                    p.bar()
                    p.v("tensor_sub", esub[:, dlt:SEG], Wsh1[:, dlt:SEG], Wc[:, 0:SEG - dlt])
                    p.bar()
                    p.act(esub[:], esub[:], AF.Exp)
                    p.bar()
                    p.v("tensor_mul", esub[:], esub[:], mask[:])
                    p.bar()
                    p.v("memset", prod1[:, 0:dlt], 0.0)
                    p.bar()
                    p.v("tensor_mul", prod1[:, dlt:SEG], fmv[:, B_, 0:SEG - dlt], esub[:, dlt:SEG])
                    p.bar()
                    banded_sc(ps_w, prod1[:], fmv[:, K_, :], True)
                    p.v("tensor_copy", babc[:, (dlt - 1) * SEG:dlt * SEG], scb[:])
                    p.bar()
                    p.v("tensor_mul", prod1[:, dlt:SEG], fmv[:, KK_, 0:SEG - dlt], esub[:, dlt:SEG])
                    p.bar()
                    banded_sc(ps_w, prod1[:], fmv[:, K_, :], True)
                    p.v("tensor_mul", prod2[:, dlt:SEG], scb[:, dlt:SEG], fmv[:, V_, 0:SEG - dlt])
                    p.bar()
                    p.v("tensor_add", base[:, dlt:SEG], base[:, dlt:SEG], prod2[:, dlt:SEG])
                    p.bar()
                for dstb, srcb in [(uloc, base), (ahat, av)]:
                    p.v("tensor_copy", dstb[:], srcb[:])
                    p.bar()
                    for l in range(1, 16):
                        for dlt in range(1, l + 1):
                            sl_t = dstb[:].rearrange("p (c k) -> p c k", k=L)[:, :, l]
                            sl_s = dstb[:].rearrange("p (c k) -> p c k", k=L)[:, :, l - dlt]
                            sl_b = babc[:, (dlt - 1) * SEG:dlt * SEG].rearrange("p (c k) -> p c k", k=L)[:, :, l]
                            p.v("tensor_mul", prod1[:, 0:SEG // L], sl_b, sl_s)
                            p.bar()
                            p.v("tensor_add", sl_t, sl_t, prod1[:, 0:SEG // L])
                            p.bar()
                for ck in range(SEG // L):
                    cs = ck * L
                    for j in range(2):
                        p.mm(ps_u[j * 64:(j + 1) * 64, :], Sb[j * 64:(j + 1) * 64, :],
                             ahat[j * 64:(j + 1) * 64, cs:cs + L], start=True, stop=True)
                    p.bar()
                    p.v("tensor_add", ubuf[:, cs:cs + L], uloc[:, cs:cs + L], ps_u[:])
                    p.bar()
                    for j in range(2):
                        p.mm(ps_u[j * 64:(j + 1) * 64, :], Sb[j * 64:(j + 1) * 64, :],
                             rv[j * 64:(j + 1) * 64, cs:cs + L], start=True, stop=True)
                    p.bar()
                    p.s("copy", ybuf[:, cs:cs + L], ps_u[:])
                    p.bar()
                    for src, dstT in [(ubuf, uT), (None, vT), (ktl, kT_), (btl, bT_)]:
                        if src is None:
                            p.t("transpose", cx.tp_ps[0:16, 0:128], fmv[:, V_, cs:cs + L], cx.eye[:])
                        else:
                            p.t("transpose", cx.tp_ps[0:16, 0:128], src[:, cs:cs + L], cx.eye[:])
                        p.bar()
                        p.s("copy", dstT[:], cx.tp_ps[0:16, 0:128])
                        p.bar()
                    p.v("tensor_scalar", Sb[:], Sb[:], e1[:, cs + L - 1:cs + L], None, OP.mult)
                    p.bar()
                    for j in range(2):
                        p.mm(ps_s[j * 64:(j + 1) * 64, :], bT_[:, j * 64:(j + 1) * 64],
                             uT[:, j * 64:(j + 1) * 64], start=True, stop=False)
                        p.mm(ps_s[j * 64:(j + 1) * 64, :], kT_[:, j * 64:(j + 1) * 64],
                             vT[:, j * 64:(j + 1) * 64], start=False, stop=True)
                    p.bar()
                    p.v("tensor_add", Sb[:], Sb[:], ps_s[:, 0:64])
                    p.bar()
                for dlt in range(16):
                    if dlt > 0:
                        p.dma("sync", mask[:], din["l3_dmask"][dlt])
                        p.bar()
                        p.v("memset", esub[:, 0:dlt], 0.0)
                        p.bar()
                        p.v("tensor_sub", esub[:, dlt:SEG], Wc[:, dlt:SEG], Wc[:, 0:SEG - dlt])
                        p.bar()
                        p.act(esub[:], esub[:], AF.Exp)
                        p.bar()
                        p.v("tensor_mul", esub[:], esub[:], mask[:])
                        p.bar()
                    for srcten, shten in [(KK_, V_), (B_, None)]:
                        if dlt == 0:
                            p.v("tensor_copy", prod1[:], fmv[:, srcten, :])
                        else:
                            p.v("memset", prod1[:, 0:dlt], 0.0)
                            p.bar()
                            p.v("tensor_mul", prod1[:, dlt:SEG], fmv[:, srcten, 0:SEG - dlt], esub[:, dlt:SEG])
                        p.bar()
                        banded_sc(ps_w, prod1[:], fmv[:, R_, :], False)
                        if shten is not None:
                            if dlt == 0:
                                p.v("tensor_mul", prod2[:], scb[:], fmv[:, V_, :])
                            else:
                                p.v("tensor_mul", prod2[:, dlt:SEG], scb[:, dlt:SEG], fmv[:, V_, 0:SEG - dlt])
                                p.bar()
                                p.v("memset", prod2[:, 0:dlt], 0.0)
                            p.bar()
                            p.v("tensor_add", ybuf[:], ybuf[:], prod2[:])
                            p.bar()
                        else:
                            if dlt == 0:
                                p.v("tensor_mul", prod2[:], scb[:], ubuf[:])
                            else:
                                p.v("tensor_mul", prod2[:, dlt:SEG], scb[:, dlt:SEG], ubuf[:, 0:SEG - dlt])
                                p.bar()
                                p.v("memset", prod2[:, 0:dlt], 0.0)
                            p.bar()
                            p.v("tensor_add", ybuf[:], ybuf[:], prod2[:])
                            p.bar()
                p.act(prod1[:], ybuf[:], AF.Square)
                p.bar()
                for dh in range(2):
                    p.mm(ps_w[0:2, :], bd64[:, 0:128:64], ybuf[:, dh * 512:(dh + 1) * 512], start=True, stop=True)
                    p.bar()
                    p.s("copy", rowsA[:, dh * 512:(dh + 1) * 512], ps_w[0:2, :])
                    p.bar()
                    p.mm(ps_w[0:2, :], bd64[:, 0:128:64], prod1[:, dh * 512:(dh + 1) * 512], start=True, stop=True)
                    p.bar()
                    p.s("copy", rowsA[:, SEG + dh * 512:SEG + (dh + 1) * 512], ps_w[0:2, :])
                    p.bar()
                mrow = rowsA[:, 0:SEG]
                m2row = rowsA[:, SEG:2 * SEG]
                rrow = rowsA[:, 2 * SEG:3 * SEG]
                p.v("tensor_scalar", mrow, mrow, 1.0 / 64, None, OP.mult)
                p.bar()
                p.v("tensor_scalar", m2row, m2row, 1.0 / 64, None, OP.mult)
                p.bar()
                p.v("tensor_mul", rrow, mrow, mrow)
                p.bar()
                p.v("tensor_sub", rrow, m2row, rrow)
                p.bar()
                p.v("tensor_scalar", rrow, rrow, 16e-5, None, OP.add)
                p.bar()
                p.act(rrow, rrow, AF.Sqrt)
                p.bar()
                p.v("reciprocal", rrow, rrow)
                p.bar()
                for dh in range(2):
                    p.mm(ps_w[:], bd2T[:], mrow[0:2, dh * 512:(dh + 1) * 512], start=True, stop=True)
                    p.bar()
                    p.v("tensor_sub", ybuf[:, dh * 512:(dh + 1) * 512], ybuf[:, dh * 512:(dh + 1) * 512], ps_w[:])
                    p.bar()
                    p.mm(ps_w[:], bd2T[:], rrow[0:2, dh * 512:(dh + 1) * 512], start=True, stop=True)
                    p.bar()
                    p.v("tensor_tensor", ybuf[:, dh * 512:(dh + 1) * 512], ybuf[:, dh * 512:(dh + 1) * 512], ps_w[:], OP.mult)
                    p.bar()
                for j in range(2):
                    rb = (2 * pg + j) * 64
                    for q in range(3):
                        p.dma("sync", gcol[j * 64:(j + 1) * 64, q:q + 1],
                              gb_my.ap()[q].unsqueeze(1)[rb:rb + 64, :])
                p.bar()
                p.v("tensor_scalar", ybuf[:], ybuf[:], gcol[:, 0:1], gcol[:, 1:2], OP.mult, OP.add)
                p.bar()
                p.v("tensor_mul", prod1[:], fmv[:, R_, :], fmv[:, K_, :])
                p.bar()
                p.v("tensor_scalar", prod1[:], prod1[:], gcol[:, 2:3], None, OP.mult)
                p.bar()
                for dh in range(2):
                    p.mm(ps_w[0:2, :], bd64[:, 0:128:64], prod1[:, dh * 512:(dh + 1) * 512], start=True, stop=True)
                    p.bar()
                    p.s("copy", rowsA[0:2, 3 * SEG + dh * 512:3 * SEG + (dh + 1) * 512], ps_w[0:2, :])
                    p.bar()
                crow = rowsA[:, 3 * SEG:4 * SEG]
                for dh in range(2):
                    p.mm(ps_w[:], bd2T[:], crow[0:2, dh * 512:(dh + 1) * 512], start=True, stop=True)
                    p.bar()
                    p.v("tensor_tensor", prod2[:, dh * 512:(dh + 1) * 512], ps_w[:], fmv[:, V_, dh * 512:(dh + 1) * 512], OP.mult)
                    p.bar()
                p.v("tensor_add", ybuf[:], ybuf[:], prod2[:])
                p.bar()
                for half in range(2):
                    p.dma("sync", yb_in.ap()[2 * seg + half, (2 * pg) * 64:(2 * pg + 2) * 64, :],
                          ybuf[:, half * 512:(half + 1) * 512])
                p.bar()
        p.cc("AllGather", OP.bypass, [yb_in.ap().opt()], [yb_ag.ap().opt()])
        p.bar()

    if _l3p < 3:
        p.dma("sync", cx.x_res[:], xres_dram.ap())
        p.bar()
        return
    with p.scope() as sc:
        wful = sc.sbuf("l3o_w", [128, NCH * D])
        wfv = wful[:].rearrange("p (c d) -> p c d", c=NCH)
        ybt = sc.sbuf("l3o_ybt", [128, NCH * 128])
        gt = sc.sbuf("l3o_gt", [128, NCH * 128])
        xrt = sc.sbuf("l3o_xrt", [128, D])
        ps_a = sc.psum("l3o_ps_a", [128, 512])
        ps_b = sc.psum("l3o_ps_b", [128, 512])
        p.dma("sync", wfv, din["l3_wrkvo"][3].rearrange("(c p) d -> p c d", p=128))
        p.bar()
        pid3 = nc.partition_id()
        for n in range(NCORES):
            p.dma("sync", yb_my.ap()[n], yb_ag.ap()[n][bass.ts(pid3, 1)].squeeze(0))
        p.bar()
        ybb = sc.sbuf("l3o_ybb", [128, 2 * 4 * 512])
        ybbv = ybb[:].rearrange("p (h g tk) -> p h g tk", h=2, g=4)
        for t in range(NT):
            b = t // TPB
            tl = t % TPB
            if tl == 0:
                for hh in range(2):
                    p.dma("sync", ybbv[:, hh, :, :],
                          yb_my.ap()[2 * b + hh].rearrange("(g p) tk -> p g tk", p=128))
                p.bar()
            for c in range(NCH):
                hh = (2 * c) // 8
                grp = (2 * c - 8 * hh) // 2
                p.v("tensor_copy", ybt[:, c * 128:(c + 1) * 128],
                    ybbv[:, hh, grp, tl * 128:(tl + 1) * 128])
            p.dma("sync", gt[:].rearrange("q (c tk) -> q c tk", c=NCH),
                  gT_dram.ap().rearrange("q (c tk) -> q c tk", c=NCH)[:, :, t * 128:(t + 1) * 128])
            p.bar()
            p.v("tensor_mul", ybt[:], ybt[:], gt[:])
            p.bar()
            for dh in range(2):
                ps = ps_a if dh == 0 else ps_b
                for c in range(NCH):
                    p.mm(ps[:], ybt[:, c * 128:(c + 1) * 128], wfv[:, c, dh * 512:(dh + 1) * 512],
                         start=(c == 0), stop=(c == NCH - 1))
            p.bar()
            p.dma("sync", xrt[:], xres_dram.ap().rearrange("p (t f) -> p t f", t=NT)[:, t, :])
            p.bar()
            xr = cx.xres_t(t)
            p.v("tensor_add", xr[:, 0:512], xrt[:, 0:512], ps_a[:])
            p.bar()
            p.v("tensor_add", xr[:, 512:1024], xrt[:, 512:1024], ps_b[:])
            p.bar()


BLOCK_FNS = [l0_attn, l0_ffn, l1_attn, l1_ffn, l2_attn, l2_ffn, l3_attn, l3_ffn]



def build_program(n_blocks=8, taps=(), dbg=False):
    nc = bass.Bass("TRN2", target_bir_lowering=False)
    cx = Ctx()
    cx.nc = nc
    din = {}

    def dram_in(name, shape):
        din[name] = nc.dram_tensor(name, list(shape), F32, kind="ExternalInput").ap()
        return din[name]

    x_in = dram_in("x_in", (TPC, D))
    dram_in("eye", (128, 128))
    dram_in("l0_wqkv", (4, D, D)); dram_in("l0_bqkv", (4, D))
    dram_in("l0f_w", (3, D, D)); dram_in("l0f_mu", (2, D))
    dram_in("l1_wqkv", (4, D, D))
    dram_in("l1f_wg", (D, 4096)); dram_in("l1f_wu", (D, 4096)); dram_in("l1f_wd", (4096, D))
    dram_in("l2_wqkv", (4, D, D))
    dram_in("l2f_wgred", (D, 16)); dram_in("l2f_wg", (8, 16))
    dram_in("l2f_w1", (8, D, 4096)); dram_in("l2f_b1", (8, 4096))
    dram_in("l2f_w2", (8, 4096, D)); dram_in("l2f_b2", (8, D))
    dram_in("l3_lambdas", (6, 1, 1, D))
    dram_in("l3_k_A", (D, 64)); dram_in("l3_k_B", (64, D))
    dram_in("l3_g_A", (D, 128)); dram_in("l3_g_B", (128, D))
    dram_in("l3_w_A", (D, 64)); dram_in("l3_w_B", (64, D))
    dram_in("l3_a_A", (D, 64)); dram_in("l3_a_B", (64, D))
    dram_in("l3_x_A", (D, 192)); dram_in("l3_x_B", (6, 32, D))
    dram_in("l3_x_time", (16, 64))
    dram_in("l3_gn_gamma", (D,)); dram_in("l3_gn_beta", (D,))
    dram_in("l3_misc", (3, D))
    dram_in("l3_wrkvo", (4, D, D))
    dram_in("l3f_wg", (D, 8)); dram_in("l3f_b1", (8, 4096)); dram_in("l3f_b2", (8, D))
    dram_in("l3f_w1", (8, D, 4096)); dram_in("l3f_w2", (8, 4096, D))
    dram_in("l3_dmask", (16, 128, 1024))
    dram_in("bd64", (128, 128))
    dram_in("bd2T", (2, 128))
    dram_in("sel_prev", (8, 1))

    x_out = nc.dram_tensor("x_out", [TPC, D], F32, kind="ExternalOutput").ap()
    dbg_f = nc.dram_tensor("dbg_f", [128, 2048], F32, kind="ExternalOutput").ap() if dbg else None
    dbg_b = nc.dram_tensor("dbg_b", [128, 4 * D], mybir.dt.bfloat16, kind="ExternalOutput").ap() if dbg else None
    tap_outs = {}
    for ti in taps:
        tap_outs[ti] = nc.dram_tensor(f"tap{ti}", [TPC, D], F32, kind="ExternalOutput").ap()

    p = Prog(nc)
    cx.p = p
    cx.din = din
    cx.dbg_f = dbg_f
    cx.dbg_b = dbg_b

    cx.x_res = p.sbuf("x_res", [128, NT * D])
    cx.eye = p.sbuf("eye_s", [128, 128])
    cx.ones_row = p.sbuf("ones_row", [1, 128])
    cx.tmp4 = p.sbuf("tmp4", [128, 4 * NT])
    cx.tp_ps = p.psum("tp_ps", [128, 512])
    cx.c_eps = p.sbuf("c_eps", [128, 1])[:]
    cx.c_invD = p.sbuf("c_invD", [128, 1])[:]

    def xres_t(t):
        return cx.x_res[:, t * D:(t + 1) * D]

    cx.xres_t = xres_t

    p.dma("sync", cx.x_res[:].rearrange("p (t f) -> p t f", t=NT),
          x_in.rearrange("(t p) f -> p t f", p=128))
    p.dma("sync", cx.eye[:], din["eye"])
    p.bar()
    p.v("memset", cx.ones_row[:], 1.0)
    p.v("memset", cx.c_eps, 1e-5)
    p.v("memset", cx.c_invD, 1.0 / D)
    p.bar()

    for bi in range(n_blocks):
        BLOCK_FNS[bi](p, cx)
        p.bar()
        if bi in tap_outs:
            p.dma("sync", tap_outs[bi].rearrange("(t p) f -> p t f", p=128),
                  cx.x_res[:].rearrange("p (t f) -> p t f", t=NT))
            p.bar()

    p.dma("sync", x_out.rearrange("(t p) f -> p t f", p=128),
          cx.x_res[:].rearrange("p (t f) -> p t f", t=NT))
    p.bar()
    return p.build(), [f"tap{i}" for i in taps]


def kernel(**inputs):
    maps = host_layout(inputs)
    nc, _ = build_program()
    r = run_bass_kernel_spmd(nc, maps, core_ids=list(range(NCORES)))
    shards = [r.results[c]["x_out"].reshape(B, PPC, D) for c in range(NCORES)]
    return np.concatenate(shards, axis=1)
